# version 25
# baseline (speedup 1.0000x reference)
import contextlib
from dataclasses import dataclass

import numpy as np
import ml_dtypes

import concourse.bass as bass
import concourse.mybir as mybir
from concourse.bass_utils import run_bass_kernel_spmd

F32 = mybir.dt.float32
BF16 = mybir.dt.bfloat16
AX = mybir.AxisListType.X
ALU = mybir.AluOpType
AF = mybir.ActivationFunctionType

ENGINES = ("tensor", "vector", "scalar", "gpsimd", "sync")
PHASES = "ABCD"
A_PARTS = "wrgl"
EXPERT_PARTS = "ged"

NEG = -1.0e9
EPS = 1.0e-5
ALPHA = 1.702
LIMIT = 7.0


@dataclass
class CFG:
    T: int = 4096
    D: int = 1024
    H: int = 16
    E: int = 8
    I: int = 2048
    TB: int = 1024
    QG: int = 4

    @property
    def NT(self): return self.T // 128
    @property
    def DC(self): return self.D // 128
    @property
    def NP(self): return self.H // 2
    @property
    def NB(self): return self.T // self.TB
    @property
    def NTS(self): return self.TB // 128
    @property
    def NIC(self): return self.I // 128
    @property
    def NSL(self): return self.I // 256
    @property
    def NQG(self): return self.NT // self.QG


def build_nc(cfg: CFG):
    nc = bass.Bass("TRN2", target_bir_lowering=False)
    T, D, H, E, I, TB = cfg.T, cfg.D, cfg.H, cfg.E, cfg.I, cfg.TB
    NT, DC, NP, NB, NTS = cfg.NT, cfg.DC, cfg.NP, cfg.NB, cfg.NTS
    NIC, NSL, QG, NQG = cfg.NIC, cfg.NSL, cfg.QG, cfg.NQG
    I2 = 2 * I
    NCH = I2 // 128
    TT5 = TB // 512
    PDW = max(D, 512)
    PGW = max(TB, 512)

    d_x = nc.dram_tensor("x", [T, D], F32, kind="ExternalInput").ap()
    d_wq = nc.dram_tensor("wq", [D, D], F32, kind="ExternalInput").ap()
    d_wk = nc.dram_tensor("wk", [D, D], F32, kind="ExternalInput").ap()
    d_wv = nc.dram_tensor("wv", [D, D], F32, kind="ExternalInput").ap()
    d_wo = nc.dram_tensor("wo", [D, D], F32, kind="ExternalInput").ap()
    d_wr = nc.dram_tensor("wr", [E, D], F32, kind="ExternalInput").ap()
    d_br = nc.dram_tensor("br", [1, E], F32, kind="ExternalInput").ap()
    d_wg = nc.dram_tensor("wg", [E, D, I2], F32, kind="ExternalInput").ap()
    d_bg = nc.dram_tensor("bg", [E, I2], F32, kind="ExternalInput").ap()
    d_wd = nc.dram_tensor("wd", [E, I, D], F32, kind="ExternalInput").ap()
    d_bd = nc.dram_tensor("bd", [E, D], F32, kind="ExternalInput").ap()
    d_ident = nc.dram_tensor("c_ident", [128, 128], F32, kind="ExternalInput").ap()
    d_masks = nc.dram_tensor("c_masks", [128, 128], F32, kind="ExternalInput").ap()
    d_maskt = nc.dram_tensor("c_maskt", [128, 128], F32, kind="ExternalInput").ap()
    d_ones = nc.dram_tensor("c_ones", [1, 128], F32, kind="ExternalInput").ap()
    d_onesb = nc.dram_tensor("c_onesb", [1, 128], BF16, kind="ExternalInput").ap()
    d_eps = nc.dram_tensor("c_eps", [128, 1], F32, kind="ExternalInput").ap()
    d_sel = nc.dram_tensor("c_sel", [2, 128], BF16, kind="ExternalInput").ap()
    d_y = nc.dram_tensor("y", [T, D], F32, kind="ExternalOutput").ap()
    d_oscr = nc.dram_tensor("o_scr", [NP * 128, T], BF16).ap()
    d_x1 = nc.dram_tensor("x1_scr", [T, D], F32).ap()
    d_wgb = nc.dram_tensor("wg_bf", [E, D, I2], BF16).ap()
    d_woscr = nc.dram_tensor("woT_scr", [128, 8 * 1024], BF16).ap()
    d_wdb = nc.dram_tensor("wd_bf", [E, I, D], BF16).ap()

    prog = {e: [] for e in ENGINES}
    cnt = {}
    waited = {e: {} for e in ENGINES}
    sems = {}

    def emit(eng, fn):
        prog[eng].append(fn)

    def bump(name, n=1):
        cnt[name] = cnt.get(name, 0) + n
        return cnt[name]

    def wait(eng, name, val=None):
        if val is None:
            val = cnt.get(name, 0)
        assert val <= cnt.get(name, 0), (eng, name, val, cnt.get(name, 0))
        if waited[eng].get(name, -1) >= val:
            return
        waited[eng][name] = val
        emit(eng, lambda e, s=sems[name], v=val: e.wait_ge(s, v))

    def run_block():
        with nc.Block() as blk:
            for eng in ENGINES:
                lst = prog[eng]
                prog[eng] = []
                if not lst:
                    continue
                def body(e, lst=lst):
                    with nc.allow_low_precision(reason="deliberate bf16 operands; fp32 accumulation"):
                        for f in lst:
                            f(e)
                getattr(blk, eng)(body)

    ctx = contextlib.ExitStack()
    EN = ctx.enter_context
    with ctx:
        for s in ("pe", "dve", "act", "pool", "ldc",
                  "xa0", "xa1", "ob0", "ob1", "xb0", "xb1",
                  "wg0", "wg1", "wg2", "wg3", "wd0", "wd1", "wpre", "wrl", "wcast",
                  "ost", "x1st0", "x1st1", "yst0", "yst1"):
            sems[s] = EN(nc.semaphore(s))

        c_ident = EN(nc.sbuf_tensor("c_ident_s", [128, 128], F32))
        c_masks = EN(nc.sbuf_tensor("c_masks_s", [128, 128], F32))
        c_maskt = EN(nc.sbuf_tensor("c_maskt_s", [128, 128], F32))
        c_ones = EN(nc.sbuf_tensor("c_ones_s", [1, 128], F32))
        c_onesb = EN(nc.sbuf_tensor("c_onesb_s", [1, 128], BF16))
        s_brb = EN(nc.sbuf_tensor("s_brb", [128, E], F32))
        s_db = EN(nc.sbuf_tensor("s_db", [E, D], F32))
        s_dbb = EN(nc.sbuf_tensor("s_dbb", [E, D], BF16))
        s_wrT = EN(nc.sbuf_tensor("s_wrT", [128, DC * E], BF16))
        s_gub = EN(nc.sbuf_tensor("s_gub", [128, E * NCH], F32))
        s_gubs = EN(nc.sbuf_tensor("s_gubs", [128, E * NCH], F32))
        s_gubp = EN(nc.sbuf_tensor("s_gubp", [128, E * NCH], F32))
        s_br_row = EN(nc.sbuf_tensor("s_br_row", [1, E], F32))
        c_eps = EN(nc.sbuf_tensor("c_eps_s", [128, 1], F32))
        c_sel = EN(nc.sbuf_tensor("c_sel_s", [2, 128], BF16))
        nc.const_aps.register(F32, EPS, c_eps[:])

        for dst, src in ((c_ident, d_ident), (c_masks, d_masks), (c_maskt, d_maskt),
                         (c_ones, d_ones), (c_onesb, d_onesb), (s_db, d_bd),
                         (s_br_row, d_br), (c_eps, d_eps), (c_sel, d_sel)):
            bump("ldc", 16)
            emit("sync", lambda e, a=dst, b=src: e.dma_start(a[:], b).then_inc(sems["ldc"], 16))
        LDC_ALL = cnt["ldc"]
        for _eng in ENGINES:
            wait(_eng, "ldc", LDC_ALL)

        with contextlib.ExitStack() as ab:
            EA = ab.enter_context
            s_ln1T = EA(nc.sbuf_tensor("s_ln1T", [128, DC * T], BF16))
            s_woT = EA(nc.sbuf_tensor("s_woT", [128, DC * D], BF16))
            s_wqT = EA(nc.sbuf_tensor("s_wqT", [128, DC * D], BF16))
            s_wkT = EA(nc.sbuf_tensor("s_wkT", [128, DC * D], BF16))
            s_wvT = EA(nc.sbuf_tensor("s_wvT", [128, DC * D], BF16))

            with contextlib.ExitStack() as aw:
                EW = aw.enter_context
                s_wst = EW(nc.sbuf_tensor("s_wst", [128, DC * D], F32))
                s_gb_st = EW(nc.sbuf_tensor("s_gb_st", [1, I2], F32))
                s_wr_st = EW(nc.sbuf_tensor("s_wr_st", [E, D], BF16))
                s_identb = EW(nc.sbuf_tensor("s_identb", [128, 128], BF16))
                p_tr = EW(nc.psum_tensor("p_trA", [128, 1024], F32))
                p_trb = EW(nc.psum_tensor("p_trb", [128, 128], BF16))

                wait("vector", "ldc", LDC_ALL)
                dv = bump("dve")
                emit("vector", lambda e: e.tensor_copy(s_identb[:], c_ident[:]).then_inc(sems["dve"], 1))
                dv = bump("dve")
                emit("vector", lambda e: e.tensor_copy(s_dbb[:], s_db[:]).then_inc(sems["dve"], 1))

                def prep_w(dsrc, dstT, tag):
                    v = bump("wpre", 16)
                    wait("sync", "pe", cnt.get("pe", 0))
                    emit("sync", lambda e, s=dsrc: e.dma_start(
                        s_wst[:].rearrange("p (c i) -> p c i", c=DC),
                        s.rearrange("(c p) i -> p c i", p=128)).then_inc(sems["wpre"], 16))
                    wait("tensor", "wpre", v)
                    wait("tensor", "ldc", LDC_ALL)
                    for oc in range(DC):
                        for ic in range(DC):
                            half = (oc * DC + ic) % 2
                            wait("tensor", "dve", cnt.get(f"wpA_h{half}", 0))
                            pv = bump("pe")
                            emit("tensor", lambda e, o=oc, i=ic, h=half: e.transpose(
                                p_tr[:, h * 512: h * 512 + 128],
                                s_wst[:, o * D + i * 128: o * D + (i + 1) * 128],
                                c_ident[:]).then_inc(sems["pe"], 1))
                            wait("vector", "pe", pv)
                            dv = bump("dve")
                            emit("vector", lambda e, o=oc, i=ic, h=half, d=dstT: e.tensor_copy(
                                d[:, i * D + o * 128: i * D + (o + 1) * 128],
                                p_tr[:, h * 512: h * 512 + 128]).then_inc(sems["dve"], 1))
                            cnt[f"wpA_h{half}"] = dv

                for dsrc, dstT, tag in (((d_wq, s_wqT, "q"), (d_wk, s_wkT, "k"),
                                         (d_wv, s_wvT, "v"), (d_wo, s_woT, "o"))
                                        if A_PARTS and "w" in A_PARTS else ()):
                    prep_w(dsrc, dstT, tag)

                if "r" in A_PARTS:
                    v = bump("wrl", 16)
                    emit("gpsimd", lambda e: e.dma_start(s_wr_st[:], d_wr).then_inc(sems["wrl"], 16))
                    wait("tensor", "wrl", v)
                    wait("tensor", "dve", cnt["dve"])
                    for ic in range(DC):
                        wait("tensor", "dve", cnt.get("wr_evac", 0))
                        pv = bump("pe")
                        emit("tensor", lambda e, i=ic: e.transpose(
                            p_trb[:, 0:E], s_wr_st[:, i * 128:(i + 1) * 128],
                            s_identb[0:E, 0:E]).then_inc(sems["pe"], 1))
                        wait("vector", "pe", pv)
                        cnt["wr_evac"] = bump("dve")
                        emit("vector", lambda e, i=ic: e.tensor_copy(
                            s_wrT[:, i * E:(i + 1) * E], p_trb[:, 0:E]).then_inc(sems["dve"], 1))

                    wait("tensor", "dve", cnt["dve"])
                    pv = bump("pe")
                    emit("tensor", lambda e: e.matmul(p_tr[:, 0:E], c_ones[:], s_br_row[:],
                                                      start=True, stop=True).then_inc(sems["pe"], 1))
                    wait("vector", "pe", pv)
                    dv = bump("dve")
                    emit("vector", lambda e: e.tensor_copy(s_brb[:], p_tr[:, 0:E]).then_inc(sems["dve"], 1))

                    wait("tensor", "dve", cnt["dve"])
                    for ei in range(E):
                        v = bump("wpre", 16)
                        wait("sync", "pe", cnt.get("pe", 0))
                        emit("sync", lambda e, ei=ei: e.dma_start(
                            s_gb_st[:], d_bg[ei:ei + 1, :]).then_inc(sems["wpre"], 16))
                        wait("tensor", "wpre", v)
                        for ch in range(NCH):
                            wait("tensor", "dve", cnt.get("gb_evac", 0))
                            pv = bump("pe")
                            emit("tensor", lambda e, c=ch: e.transpose(
                                p_tr[:, 0:1], s_gb_st[:, c * 128:(c + 1) * 128],
                                c_ident[0:1, 0:1]).then_inc(sems["pe"], 1))
                            wait("vector", "pe", pv)
                            cnt["gb_evac"] = bump("dve")
                            emit("vector", lambda e, ei=ei, c=ch: e.tensor_copy(
                                s_gub[:, ei * NCH + c: ei * NCH + c + 1], p_tr[:, 0:1]).then_inc(sems["dve"], 1))
                    wait("vector", "dve", cnt["dve"])
                    dv = bump("dve")
                    emit("vector", lambda e: e.tensor_scalar(s_gubs[:], s_gub[:], ALPHA, None,
                                                             ALU.mult).then_inc(sems["dve"], 1))
                    dv = bump("dve")
                    emit("vector", lambda e: e.tensor_scalar(s_gubp[:], s_gub[:], 1.0, None,
                                                             ALU.add).then_inc(sems["dve"], 1))
                W_PREP_DVE = cnt["dve"]

            ln1_ctx = contextlib.ExitStack()
            E1 = ln1_ctx.enter_context
            s_xa = [E1(nc.sbuf_tensor(f"s_xa{i}", [128, D], F32, side="right")) for i in range(2)]
            s_xn = [E1(nc.sbuf_tensor(f"s_xn{i}", [128, D], F32, side="right")) for i in range(2)]
            s_sq = E1(nc.sbuf_tensor("s_sq", [128, D], BF16, side="right"))
            s_st = E1(nc.sbuf_tensor("s_st", [128, 16], F32, side="right"))
            p_ln = E1(nc.psum_tensor("p_ln", [128, 1024], F32))

            def ln_stats(src_ap, dcols, sq, st):
                wait("scalar", "dve", cnt["dve"])
                wait("vector", "act", cnt["act"] if "act" in cnt else 0)
                av = bump("act")
                emit("scalar", lambda e, s=src_ap: e.activation(
                    sq[:, 0:dcols], s, AF.Square, accum_out=st[:, 1:2]).then_inc(sems["act"], 1))
                dv = bump("dve")
                emit("vector", lambda e, s=src_ap: e.reduce_sum(
                    st[:, 0:1], s, axis=AX).then_inc(sems["dve"], 1))
                wait("vector", "dve", dv)
                wait("vector", "act", av)
                dv = bump("dve")
                emit("vector", lambda e: e.tensor_scalar(st[:, 2:3], st[:, 0:1], 1.0 / dcols,
                                                         None, ALU.mult).then_inc(sems["dve"], 1))
                wait("vector", "dve", dv)
                dv = bump("dve")
                emit("vector", lambda e: e.scalar_tensor_tensor(
                    st[:, 3:4], st[:, 2:3], -1.0, st[:, 2:3], ALU.mult, ALU.mult).then_inc(sems["dve"], 1))
                wait("vector", "dve", dv)
                dv = bump("dve")
                emit("vector", lambda e: e.scalar_tensor_tensor(
                    st[:, 3:4], st[:, 1:2], 1.0 / dcols, st[:, 3:4], ALU.mult, ALU.add).then_inc(sems["dve"], 1))
                wait("scalar", "dve", dv)
                av = bump("act")
                emit("scalar", lambda e: e.activation(st[:, 4:5], st[:, 3:4], AF.Ln,
                                                      bias=EPS).then_inc(sems["act"], 1))
                wait("scalar", "act", av)
                av = bump("act")
                emit("scalar", lambda e: e.activation(st[:, 3:4], st[:, 4:5], AF.Exp,
                                                      scale=-0.5).then_inc(sems["act"], 1))
                return av

            wait("sync", "pe", cnt["pe"])
            wait("sync", "dve", cnt["dve"])
            for _eng in ("vector", "scalar", "sync"):
                wait(_eng, "wpre", cnt.get("wpre", 0))
            for tt in range(NT if "l" in A_PARTS else 0):
                sl = tt % 2
                wait("sync", "dve", cnt.get(f"xa_free{sl}", 0))
                xv = bump(f"xa{sl}", 16)
                emit("sync", lambda e, t=tt, s=sl, xa=s_xa[sl]: e.dma_start(
                    xa[:], d_x[t * 128:(t + 1) * 128, :]).then_inc(sems[f"xa{s}"], 16))
                wait("scalar", f"xa{sl}", xv)
                wait("vector", f"xa{sl}", xv)
                av = ln_stats(s_xa[sl][:], D, s_sq, s_st)
                wait("vector", "act", av)
                wait("vector", "pe", cnt.get(f"xn_free{sl}", 0))
                dv = bump("dve")
                emit("vector", lambda e, xn=s_xn[sl], xa=s_xa[sl], st=s_st: e.tensor_scalar(
                    xn[:], xa[:], st[:, 2:3], st[:, 3:4],
                    ALU.subtract, ALU.mult).then_inc(sems["dve"], 1))
                cnt[f"xa_free{sl}"] = dv
                wait("tensor", "dve", dv)
                for ic in range(DC):
                    half = ic % 2
                    wait("tensor", "dve", cnt.get(f"ln_h{half}", 0))
                    pv = bump("pe")
                    emit("tensor", lambda e, i=ic, h=half, xn=s_xn[sl]: e.transpose(
                        p_ln[:, h * 512: h * 512 + 128], xn[:, i * 128:(i + 1) * 128],
                        c_ident[:]).then_inc(sems["pe"], 1))
                    wait("vector", "pe", pv)
                    dv = bump("dve")
                    emit("vector", lambda e, t=tt, i=ic, h=half: e.tensor_copy(
                        s_ln1T[:, i * T + t * 128: i * T + (t + 1) * 128],
                        p_ln[:, h * 512: h * 512 + 128]).then_inc(sems["dve"], 1))
                    cnt[f"ln_h{half}"] = dv
                cnt[f"xn_free{sl}"] = cnt["pe"]

            ln1_ctx.close()
            wait("gpsimd", "wpre", cnt.get("wpre", 0))
            for ei in range(E):
                bump("wcast", 16)
                emit("gpsimd", lambda e, ei=ei: e.dma_start(d_wgb[ei], d_wg[ei]).then_inc(sems["wcast"], 16))
                bump("wcast", 16)
                emit("gpsimd", lambda e, ei=ei: e.dma_start(d_wdb[ei], d_wd[ei]).then_inc(sems["wcast"], 16))
            cnt["WCAST_ALL"] = cnt["wcast"]
            for _bi in range(4):
                cnt[f"b{_bi}_dve"] = cnt.get("dve", 0)

            bctx = contextlib.ExitStack()
            EB = bctx.enter_context
            s_Ka_ = [EB(nc.sbuf_tensor(f"s_Ka{i}", [65, T], BF16)) for i in range(2)]
            s_Kb_ = [EB(nc.sbuf_tensor(f"s_Kb{i}", [65, T], BF16)) for i in range(2)]
            s_Va_ = [EB(nc.sbuf_tensor(f"s_Va{i}", [128, NT * 65], BF16)) for i in range(2)]
            s_Vb_ = [EB(nc.sbuf_tensor(f"s_Vb{i}", [128, NT * 65], BF16)) for i in range(2)]
            s_Qa_ = [EB(nc.sbuf_tensor(f"s_Qa{i}", [65, QG * 128], BF16)) for i in range(2)]
            s_Qb_ = [EB(nc.sbuf_tensor(f"s_Qb{i}", [65, QG * 128], BF16)) for i in range(2)]
            s_PT = [EB(nc.sbuf_tensor(f"s_PT{i}", [128, QG * 128], BF16)) for i in range(3)]
            s_Op = EB(nc.sbuf_tensor("s_Op", [128, T], BF16))
            s_mst = EB(nc.sbuf_tensor("s_mst", [128, 64], F32))
            s_m = EB(nc.sbuf_tensor("s_m", [128, 4 * QG], F32))
            s_rl = [EB(nc.sbuf_tensor(f"s_rl{i}", [1, QG * 128], BF16)) for i in range(2)]
            s_bc = EB(nc.sbuf_tensor("s_bc", [128, QG * 128], BF16))

            p_s = EB(nc.psum_tensor("p_s", [128, 1024], F32))
            p_st = EB(nc.psum_tensor("p_st", [128, 1024], F32))
            p_oa = EB(nc.psum_tensor("p_oa", [65, QG * 128], F32))
            p_ob = EB(nc.psum_tensor("p_ob", [65, QG * 128], F32))
            p_bc = EB(nc.psum_tensor("p_bc", [128, QG * 128], F32))
            p_misc = EB(nc.psum_tensor("p_misc", [128, 512], F32))

            NSP = T // 512
            P1B = [(p_s, 0), (p_s, 512), (p_st, 0), (p_st, 512)]

            WARM = 0

            def warm_pe(n=None):
                for _ in range(WARM if n is None else n):
                    pv = bump("pe")
                    emit("tensor", lambda e: e.matmul(
                        p_misc[:, 0:512], s_wqT[:, 0:128], s_ln1T[:, 0:512],
                        start=True, stop=True).then_inc(sems["pe"], 1))

            def bank_acquire(warm=0):
                warm_pe(warm)
                bi = cnt.get("rot", 0) % 4
                cnt["rot"] = cnt.get("rot", 0) + 1
                wait("tensor", "dve", cnt.get(f"b{bi}_dve", 0))
                wait("tensor", "act", cnt.get(f"b{bi}_act", 0))
                return bi, P1B[bi]

            if "B" in PHASES:
                wait("vector", "dve", cnt["dve"])
                for par in range(2):
                    for dst in (s_Ka_[par], s_Kb_[par]):
                        dv = bump("dve")
                        emit("vector", lambda e, d=dst: e.memset(d[64:65, :], 1.0).then_inc(sems["dve"], 1))
                    for dst in (s_Va_[par], s_Vb_[par]):
                        for ts in range(NT):
                            dv = bump("dve")
                            emit("vector", lambda e, d=dst, t=ts: e.memset(
                                d[:, t * 65 + 64: t * 65 + 65], 1.0).then_inc(sems["dve"], 1))

            def proj_tasks(pp):
                par = pp % 2
                tasks = []

                def k_task(tsp, pp=pp, par=par):
                    bi, (pb, po) = bank_acquire()
                    for ic in range(DC):
                        pv = bump("pe")
                        emit("tensor", lambda e, p=pp, i=ic, t=tsp, pb=pb, po=po: e.matmul(
                            pb[:, po: po + 512],
                            s_wkT[:, i * D + p * 128: i * D + (p + 1) * 128],
                            s_ln1T[:, i * T + t * 512: i * T + (t + 1) * 512],
                            start=(i == 0), stop=(i == DC - 1)).then_inc(sems["pe"], 1))
                    wait("scalar", "pe", cnt["pe"])
                    for dst, lo in ((s_Ka_[par], 0), (s_Kb_[par], 64)):
                        av = bump("act")
                        emit("scalar", lambda e, d=dst, l=lo, t=tsp, pb=pb, po=po: e.copy(
                            d[0:64, t * 512:(t + 1) * 512],
                            pb[l:l + 64, po: po + 512]).then_inc(sems["act"], 1))
                    cnt[f"b{bi}_act"] = cnt["act"]
                    cnt[f"kv_act{par}"] = cnt["act"]

                def v_task(ts, pp=pp, par=par):
                    bi, (pb, po) = bank_acquire()
                    for ic in range(DC):
                        pv = bump("pe")
                        emit("tensor", lambda e, p=pp, i=ic, t=ts, pb=pb, po=po: e.matmul(
                            pb[:, po: po + 128],
                            s_ln1T[:, i * T + t * 128: i * T + (t + 1) * 128],
                            s_wvT[:, i * D + p * 128: i * D + (p + 1) * 128],
                            start=(i == 0), stop=(i == DC - 1)).then_inc(sems["pe"], 1))
                    wait("scalar", "pe", cnt["pe"])
                    for dst, lo in ((s_Va_[par], 0), (s_Vb_[par], 64)):
                        av = bump("act")
                        emit("scalar", lambda e, d=dst, l=lo, t=ts, pb=pb, po=po: e.copy(
                            d[:, t * 65: t * 65 + 64],
                            pb[:, po + l: po + l + 64]).then_inc(sems["act"], 1))
                    cnt[f"b{bi}_act"] = cnt["act"]
                    cnt[f"kv_act{par}"] = cnt["act"]

                for tsp in range(NSP):
                    tasks.append(lambda tsp=tsp: k_task(tsp))
                for ts in range(NT):
                    tasks.append(lambda ts=ts: v_task(ts))
                return tasks

            proj_queue = []

            def drain_proj(n=1):
                for _ in range(n):
                    if proj_queue:
                        proj_queue.pop(0)()

            for pr in range(NP if "B" in PHASES else 0):
                par = pr % 2
                s_Ka, s_Kb = s_Ka_[par], s_Kb_[par]
                s_Va, s_Vb = s_Va_[par], s_Vb_[par]
                if pr == 0:
                    for t_ in proj_tasks(0):
                        t_()
                    cnt["kv_ready0"] = cnt["act"]
                if pr + 1 < NP:
                    proj_queue.extend(proj_tasks(pr + 1))
                wait("tensor", "act", cnt[f"kv_ready{par}"])
                VD = cnt["dve"]

                s1_queue = []
                pending_l = []

                def queue_lchain(qg0):
                    wait("vector", "pe", cnt["avp_1"])
                    wait("vector", "dve", cnt["dve"])
                    for hh, p_o in enumerate((p_oa, p_ob)):
                        dv = bump("dve")
                        emit("vector", lambda e, p_o=p_o, hh=hh: e.reciprocal(
                            s_rl[hh][:], p_o[64:65, :]).then_inc(sems["dve"], 1))
                    pending_l.append((qg0, cnt["dve"]))

                def emit_lchain():
                    if not pending_l:
                        return
                    (qg0, rlv) = pending_l.pop(0)
                    wait("tensor", "dve", rlv)
                    wait("tensor", "dve", cnt.get("bc_used", 0))
                    wait("tensor", "dve", cnt.get("mt_used", 0))
                    for hh in range(2):
                        pv = bump("pe")
                        emit("tensor", lambda e, hh=hh: e.matmul(
                            p_bc[hh * 64:(hh + 1) * 64, :], c_onesb[:, 0:64], s_rl[hh][:],
                            start=True, stop=True).then_inc(sems["pe"], 1))
                    wait("vector", "pe", cnt["pe"])
                    dv = bump("dve")
                    emit("vector", lambda e: e.tensor_copy(s_bc[:], p_bc[:]).then_inc(sems["dve"], 1))
                    cnt["bc_used"] = dv
                    wait("vector", "dve", dv)
                    wait("vector", "ost", cnt.get("op_reuse", 0))
                    for hh, p_o in enumerate((p_oa, p_ob)):
                        dv = bump("dve")
                        emit("vector", lambda e, p_o=p_o, hh=hh, qg=qg0: e.tensor_tensor(
                            s_Op[hh * 64:(hh + 1) * 64, qg * QG * 128:(qg + 1) * QG * 128],
                            p_o[0:64, :], s_bc[hh * 64:(hh + 1) * 64, :], ALU.mult).then_inc(sems["dve"], 1))
                        cnt[f"o_free_{hh}"] = dv

                def drain_any(n=1):
                    for _ in range(n):
                        if s1_queue:
                            s1_queue.pop(0)()
                        elif proj_queue:
                            proj_queue.pop(0)()

                def stage1_tasks(qg, pr):
                    par = qg % 2
                    sQa, sQb = s_Qa_[par], s_Qb_[par]
                    tasks = []

                    def qproj(qg=qg, pr=pr, par=par, sQa=sQa, sQb=sQb):
                        qslots = []
                        for j in range(QG):
                            qt = qg * QG + j
                            bi, (pb, po) = bank_acquire()
                            for ic in range(DC):
                                pv = bump("pe")
                                emit("tensor", lambda e, p=pr, i=ic, q=qt, pb=pb, po=po: e.matmul(
                                    pb[:, po: po + 128],
                                    s_wqT[:, i * D + p * 128: i * D + (p + 1) * 128],
                                    s_ln1T[:, i * T + q * 128: i * T + (q + 1) * 128],
                                    start=(i == 0), stop=(i == DC - 1)).then_inc(sems["pe"], 1))
                            qslots.append((bi, pb, po, cnt["pe"]))
                        wait("scalar", "pe", cnt.get(f"qa_used{par}", 0))
                        for j in range(QG):
                            bi, pb, po, pev = qslots[j]
                            wait("scalar", "pe", pev)
                            for dst, lo in ((sQa, 0), (sQb, 64)):
                                av = bump("act")
                                emit("scalar", lambda e, d=dst, l=lo, j=j, pb=pb, po=po: e.activation(
                                    d[0:64, j * 128:(j + 1) * 128], pb[l:l + 64, po: po + 128],
                                    AF.Copy, scale=0.125).then_inc(sems["act"], 1))
                            cnt[f"b{bi}_act"] = cnt["act"]
                        cnt[f"qev{par}"] = cnt["act"]

                    tasks.append(qproj)

                    def p1unit(j, hh, qg=qg, par=par):
                        qt = qg * QG + j
                        sQ = (s_Qa_, s_Qb_)[hh][par]
                        sK = (s_Ka_, s_Kb_)[hh][pr % 2]
                        n512 = qt // 4
                        rem = qt - 4 * n512
                        nred = 0
                        wait("tensor", "act", cnt[f"qev{par}"])
                        for ksp in range(n512 + 1):
                            bi, (pb, po) = bank_acquire()
                            last = ksp == n512
                            w = (rem + 1) * 128 if last else 512
                            k0 = ksp * 512
                            pv = bump("pe")
                            emit("tensor", lambda e, sQ=sQ, sK=sK, j=j, k0=k0, w=w,
                                 pb=pb, po=po: e.matmul(
                                pb[:, po: po + w],
                                sQ[0:64, j * 128:(j + 1) * 128],
                                sK[0:64, k0: k0 + w],
                                start=True, stop=True).then_inc(sems["pe"], 1))
                            mcol = (par * 32) + (j % 2) * 16 + hh * 8 + nred
                            wait("vector", "pe", pv)
                            if last:
                                dv = bump("dve")
                                emit("vector", lambda e, r=rem, pb=pb, po=po: e.tensor_tensor(
                                    pb[:, po + r * 128: po + (r + 1) * 128],
                                    pb[:, po + r * 128: po + (r + 1) * 128],
                                    c_masks[:], ALU.add).then_inc(sems["dve"], 1))
                                wait("vector", "dve", dv)
                            dv = bump("dve")
                            emit("vector", lambda e, n=mcol, w=w, pb=pb, po=po: e.reduce_max(
                                s_mst[:, n:n + 1], pb[:, po: po + w],
                                axis=AX).then_inc(sems["dve"], 1))
                            cnt[f"b{bi}_dve"] = dv
                            nred += 1
                        wait("vector", "dve", cnt["dve"])
                        base = (par * 32) + (j % 2) * 16 + hh * 8
                        dv = bump("dve")
                        emit("vector", lambda e, b=base, n=nred, j=j, hh=hh, par=par: e.reduce_max(
                            s_m[:, par * 2 * QG + hh * QG + j: par * 2 * QG + hh * QG + j + 1],
                            s_mst[:, b: b + n], axis=AX).then_inc(sems["dve"], 1))
                        cnt[f"comb_{par}_{hh}_{j}"] = dv

                    def mfin(j, hh, par=par):
                        sQ = (s_Qa_, s_Qb_)[hh][par]
                        slot = (hh * QG + j) % 4
                        wait("tensor", "dve", cnt[f"comb_{par}_{hh}_{j}"])
                        wait("tensor", "dve", cnt.get("bc_used", 0))
                        wait("tensor", "dve", cnt.get(f"mt_slot{slot}", 0))
                        pv = bump("pe")
                        emit("tensor", lambda e, j=j, hh=hh, par=par, sl=slot: e.transpose(
                            p_bc[0:1, sl * 128:(sl + 1) * 128],
                            s_m[:, par * 2 * QG + hh * QG + j: par * 2 * QG + hh * QG + j + 1],
                            c_ident[:]).then_inc(sems["pe"], 1))
                        wait("vector", "pe", pv)
                        dv = bump("dve")
                        emit("vector", lambda e, d=sQ, j=j, sl=slot: e.tensor_scalar(
                            d[64:65, j * 128:(j + 1) * 128], p_bc[0:1, sl * 128:(sl + 1) * 128],
                            -1.0, None, ALU.mult).then_inc(sems["dve"], 1))
                        cnt[f"mt_slot{slot}"] = dv
                        cnt[f"mdone{par}"] = dv
                        cnt["mt_used"] = dv

                    for j in range(QG):
                        for hh in range(2):
                            tasks.append(lambda j=j, hh=hh: p1unit(j, hh))
                    for j in range(QG):
                        for hh in range(2):
                            tasks.append(lambda j=j, hh=hh: mfin(j, hh))
                    return tasks

                def stage2(qg, pr):
                    par = qg % 2
                    qt_hi = qg * QG + QG - 1
                    emit_lchain()
                    for hh in range(2):
                        sQ = (s_Qa_, s_Qb_)[hh][par]
                        sK = (s_Ka_, s_Kb_)[hh][pr % 2]
                        sV = (s_Va_, s_Vb_)[hh][pr % 2]
                        p_o = (p_oa, p_ob)[hh]
                        wait("tensor", "dve", cnt.get(f"mdone{par}", 0))
                        wait("tensor", "dve", cnt.get(f"o_free_{hh}", 0))
                        pend = None
                        for kt in range(qt_hi + 1):
                            jstart = max(0, kt - qg * QG)
                            span = (QG - jstart) * 128
                            ring = kt % 3
                            bi, (pb, po) = bank_acquire()
                            pv = bump("pe")
                            emit("tensor", lambda e, sQ=sQ, sK=sK, k=kt, js=jstart, w=span,
                                 pb=pb, po=po: e.matmul(
                                pb[:, po: po + w],
                                sK[:, k * 128:(k + 1) * 128],
                                sQ[:, js * 128: js * 128 + w],
                                start=True, stop=True).then_inc(sems["pe"], 1))
                            mask_dve = None
                            if kt >= qg * QG:
                                jd = kt - qg * QG
                                wait("vector", "pe", pv)
                                mask_dve = bump("dve")
                                emit("vector", lambda e, jd=jd, js=jstart, pb=pb, po=po: e.tensor_tensor(
                                    pb[:, po + (jd - js) * 128: po + (jd - js + 1) * 128],
                                    pb[:, po + (jd - js) * 128: po + (jd - js + 1) * 128],
                                    c_maskt[:], ALU.add).then_inc(sems["dve"], 1))
                                cnt[f"b{bi}_dve"] = mask_dve
                            wait("scalar", "pe", pv)
                            if mask_dve is not None:
                                wait("scalar", "dve", mask_dve)
                            wait("scalar", "pe", cnt.get(f"pt_used{ring}", 0))
                            av = bump("act")
                            emit("scalar", lambda e, r=ring, w=span, pb=pb, po=po: e.activation(
                                s_PT[r][:, 0:w], pb[:, po: po + w],
                                AF.Exp).then_inc(sems["act"], 1))
                            cnt[f"b{bi}_act"] = av
                            if pend is not None:
                                k2, js2, w2, r2, av2 = pend
                                wait("tensor", "act", av2)
                                pv = bump("pe")
                                emit("tensor", lambda e, sV=sV, p_o=p_o, k=k2, js=js2, w=w2, r=r2,
                                     st=(k2 == 0), sp=(k2 == qt_hi): e.matmul(
                                    p_o[0:65, js * 128: js * 128 + w],
                                    sV[:, k * 65:(k + 1) * 65],
                                    s_PT[r][:, 0:w],
                                    start=st, stop=sp).then_inc(sems["pe"], 1))
                                cnt[f"pt_used{r2}"] = pv
                            pend = (kt, jstart, span, ring, av)
                            if kt % 3 == 2:
                                drain_any(1)
                        k2, js2, w2, r2, av2 = pend
                        wait("tensor", "act", av2)
                        pv = bump("pe")
                        emit("tensor", lambda e, sV=sV, p_o=p_o, k=k2, js=js2, w=w2, r=r2,
                             st=(k2 == 0): e.matmul(
                            p_o[0:65, js * 128: js * 128 + w],
                            sV[:, k * 65:(k + 1) * 65],
                            s_PT[r][:, 0:w],
                            start=st, stop=True).then_inc(sems["pe"], 1))
                        cnt[f"pt_used{r2}"] = pv
                        cnt[f"avp_{hh}"] = pv
                        if hh == 1:
                            queue_lchain(qg)
                        drain_any(1)
                    cnt[f"qa_used{par}"] = cnt["pe"]

                for t_ in stage1_tasks(0, pr):
                    t_()
                for qg in range(NQG):
                    if qg + 1 < NQG:
                        s1_queue.extend(stage1_tasks(qg + 1, pr))
                    stage2(qg, pr)
                    while s1_queue:
                        s1_queue.pop(0)()
                drain_proj(len(proj_queue))
                if pr + 1 < NP:
                    cnt[f"kv_ready{(pr + 1) % 2}"] = cnt["act"]
                emit_lchain()
                wait("scalar", "dve", cnt["dve"])
                sv = bump("ost", 16)
                emit("scalar", lambda e, p=pr: e.dma_start(
                    d_oscr[p * 128:(p + 1) * 128, :], s_Op[:]).then_inc(sems["ost"], 16))
                cnt["op_reuse"] = sv
            OST_ALL = cnt.get("ost", 0)

            bctx.close()

            wait("sync", "pe", cnt.get("pe", 0))
            wait("sync", "dve", cnt.get("dve", 0))
            wait("sync", "act", cnt.get("act", 0))
            wait("sync", "wrl", cnt.get("wrl", 0))
            s_ob = [EA(nc.sbuf_tensor(f"s_ob{i}", [128, DC * 128], BF16)) for i in range(2)]
            s_x1o = [EA(nc.sbuf_tensor(f"s_x1o{i}", [128, D], F32)) for i in range(2)]
            s_xa = [EA(nc.sbuf_tensor(f"s_xac{i}", [128, D], F32)) for i in range(2)]
            s_sq = EA(nc.sbuf_tensor("s_sqc", [128, D], BF16))
            s_st = EA(nc.sbuf_tensor("s_stc", [128, 16], F32))
            p_wo = EA(nc.psum_tensor("p_wo", [128, 2 * PDW], F32))

            d_oscr_r = d_oscr.rearrange("(c p) t -> p c t", p=128)
            for ts in range(NT if "C" in PHASES else 0):
                sl = ts % 2
                half = ts % 2
                wait("sync", "ost", OST_ALL)
                wait("sync", "pe", cnt.get(f"ob_free{sl}", 0))
                ov = bump(f"ob{sl}", 16)
                emit("sync", lambda e, t=ts, s=sl: e.dma_start(
                    s_ob[s][:].rearrange("p (c t) -> p c t", c=DC),
                    d_oscr_r[:, :, t * 128:(t + 1) * 128]).then_inc(sems[f"ob{s}"], 16))
                wait("sync", "dve", cnt.get(f"xa_free{sl}", 0))
                xv = bump(f"xa{sl}", 16)
                emit("sync", lambda e, t=ts, s=sl: e.dma_start(
                    s_xa[s][:], d_x[t * 128:(t + 1) * 128, :]).then_inc(sems[f"xa{s}"], 16))

                wait("tensor", f"ob{sl}", ov)
                wait("tensor", "dve", cnt.get(f"wo_h{half}", 0))
                wait("tensor", "act", cnt.get(f"wo_a{half}", 0))
                for ic in range(DC):
                    for n0 in range(0, D, 512):
                        w5 = min(512, D - n0)
                        pv = bump("pe")
                        emit("tensor", lambda e, s=sl, i=ic, n=n0, w=w5, h=half: e.matmul(
                            p_wo[:, h * PDW + n: h * PDW + n + w],
                            s_ob[s][:, i * 128:(i + 1) * 128],
                            s_woT[:, i * D + n: i * D + n + w],
                            start=(i == 0), stop=(i == DC - 1)).then_inc(sems["pe"], 1))
                PEWO = cnt["pe"]
                cnt[f"ob_free{sl}"] = PEWO
                wait("scalar", "pe", PEWO)
                wait("scalar", "dve", cnt["dve"])
                av = bump("act")
                emit("scalar", lambda e, h=half: e.activation(
                    s_sq[:, 0:D], p_wo[:, h * PDW: h * PDW + D], AF.Square,
                    accum_out=s_st[:, 1:2]).then_inc(sems["act"], 1))
                cnt[f"wo_a{half}"] = av
                wait("vector", "act", av)
                wait("vector", "pe", PEWO)
                dv = bump("dve")
                emit("vector", lambda e, h=half: e.reduce_sum(
                    s_st[:, 0:1], p_wo[:, h * PDW: h * PDW + D], axis=AX).then_inc(sems["dve"], 1))
                wait("vector", "dve", dv)
                dv = bump("dve")
                emit("vector", lambda e: e.tensor_scalar(s_st[:, 2:3], s_st[:, 0:1], 1.0 / D,
                                                         None, ALU.mult).then_inc(sems["dve"], 1))
                wait("vector", "dve", dv)
                dv = bump("dve")
                emit("vector", lambda e: e.scalar_tensor_tensor(
                    s_st[:, 3:4], s_st[:, 2:3], -1.0, s_st[:, 2:3], ALU.mult, ALU.mult).then_inc(sems["dve"], 1))
                wait("vector", "dve", dv)
                dv = bump("dve")
                emit("vector", lambda e: e.scalar_tensor_tensor(
                    s_st[:, 3:4], s_st[:, 1:2], 1.0 / D, s_st[:, 3:4], ALU.mult, ALU.add).then_inc(sems["dve"], 1))
                wait("scalar", "dve", dv)
                av = bump("act")
                emit("scalar", lambda e: e.activation(s_st[:, 4:5], s_st[:, 3:4], AF.Ln,
                                                      bias=EPS).then_inc(sems["act"], 1))
                wait("scalar", "act", av)
                av = bump("act")
                emit("scalar", lambda e: e.activation(s_st[:, 3:4], s_st[:, 4:5], AF.Exp,
                                                      scale=-0.5).then_inc(sems["act"], 1))
                wait("vector", "act", av)
                wait("vector", f"xa{sl}", xv)
                wait("vector", f"x1st{sl}", cnt.get(f"x1o_free{sl}", 0))
                dv = bump("dve")
                emit("vector", lambda e, s=sl, h=half: e.tensor_scalar(
                    s_x1o[s][:], p_wo[:, h * PDW: h * PDW + D], s_st[:, 2:3], s_st[:, 3:4],
                    ALU.subtract, ALU.mult).then_inc(sems["dve"], 1))
                cnt[f"wo_h{half}"] = dv
                wait("vector", "dve", dv)
                dv = bump("dve")
                emit("vector", lambda e, s=sl: e.tensor_tensor(
                    s_x1o[s][:], s_x1o[s][:], s_xa[s][:], ALU.add).then_inc(sems["dve"], 1))
                cnt[f"xa_free{sl}"] = dv
                wait("scalar", "dve", dv)
                xsv = bump(f"x1st{sl}", 16)
                emit("scalar", lambda e, t=ts, s=sl: e.dma_start(
                    d_x1[t * 128:(t + 1) * 128, :], s_x1o[s][:]).then_inc(sems[f"x1st{s}"], 16))
                cnt[f"x1o_free{sl}"] = xsv
            X1ST_ALL = (cnt.get("x1st0", 0), cnt.get("x1st1", 0))

            if "D" not in PHASES:
                wait("scalar", "dve", cnt.get("dve", 0))
                wait("scalar", "act", cnt.get("act", 0))
                yv = bump("yst0", 16)
                emit("scalar", lambda e: e.dma_start(
                    d_y[0:128, :], s_xa[0][:]).then_inc(sems["yst0"], 16))
                wait("sync", "yst0", yv)
                wait("sync", "ost", OST_ALL)
                wait("sync", "x1st0", X1ST_ALL[0])
                wait("sync", "x1st1", X1ST_ALL[1])
                run_block()
                return nc

            run_block()

        with contextlib.ExitStack() as dd:
            ED = dd.enter_context
            s_ln2T = ED(nc.sbuf_tensor("s_ln2T", [128, DC * TB], BF16))
            s_acc = ED(nc.sbuf_tensor("s_acc", [128, NTS * D], F32))
            s_act = ED(nc.sbuf_tensor("s_act", [128, NIC * TB], BF16))
            s_wg = [ED(nc.sbuf_tensor(f"s_wg{i}", [128, DC * 512], BF16)) for i in range(4)]
            s_wd = ED(nc.sbuf_tensor("s_wdn", [128, NIC * D], BF16))
            s_x1i = [ED(nc.sbuf_tensor(f"s_x1i{i}", [128, D], F32)) for i in range(2)]
            s_out = [ED(nc.sbuf_tensor(f"s_out{i}", [128, D], F32)) for i in range(2)]
            s_xn2_ = [ED(nc.sbuf_tensor(f"s_xn2{i}", [128, D], F32)) for i in range(2)]
            s_sq2_ = [ED(nc.sbuf_tensor(f"s_sq2{i}", [128, D], BF16)) for i in range(2)]
            s_st2 = ED(nc.sbuf_tensor("s_st2", [128, 32], F32))
            s_sig = ED(nc.sbuf_tensor("s_sig", [128, TB], F32))
            s_glu = [ED(nc.sbuf_tensor(f"s_glu{i}", [128, TB], F32)) for i in range(2)]
            s_uc = ED(nc.sbuf_tensor("s_uc", [128, TB], F32))
            s_w8 = ED(nc.sbuf_tensor("s_w8", [128, NTS * E], F32))
            s_wt8 = ED(nc.sbuf_tensor("s_wt8", [E, NTS * 128], BF16))
            s_sm = ED(nc.sbuf_tensor("s_sm", [128, 3 * E + 4], F32))
            p_g = ED(nc.psum_tensor("p_g", [128, 2048], F32))
            p_d = ED(nc.psum_tensor("p_d", [128, 2 * PDW], F32))

            for _eng in ("gpsimd", "sync", "scalar"):
                wait(_eng, "x1st0", X1ST_ALL[0])
                wait(_eng, "x1st1", X1ST_ALL[1])
                wait(_eng, "ost", OST_ALL)
            wait("sync", "wcast", cnt.get("WCAST_ALL", 0))
            for _s in ("dve", "act", "pool", "pe"):
                wait("tensor", _s, cnt.get(_s, 0))
                wait("vector", _s, cnt.get(_s, 0))
                wait("scalar", _s, cnt.get(_s, 0))
                wait("gpsimd", _s, cnt.get(_s, 0))

            def ln_norm(src_ap, dst_ap, dcols, sq, st):
                wait("scalar", "dve", cnt["dve"])
                wait("vector", "act", cnt["act"])
                av = bump("act")
                emit("scalar", lambda e, s=src_ap: e.activation(
                    sq[:, 0:dcols], s, AF.Square, accum_out=st[:, 1:2]).then_inc(sems["act"], 1))
                dv = bump("dve")
                emit("vector", lambda e, s=src_ap: e.reduce_sum(st[:, 0:1], s, axis=AX).then_inc(sems["dve"], 1))
                wait("vector", "dve", dv)
                wait("vector", "act", av)
                dv = bump("dve")
                emit("vector", lambda e: e.tensor_scalar(st[:, 2:3], st[:, 0:1], 1.0 / dcols,
                                                         None, ALU.mult).then_inc(sems["dve"], 1))
                wait("vector", "dve", dv)
                dv = bump("dve")
                emit("vector", lambda e: e.scalar_tensor_tensor(
                    st[:, 3:4], st[:, 2:3], -1.0, st[:, 2:3], ALU.mult, ALU.mult).then_inc(sems["dve"], 1))
                wait("vector", "dve", dv)
                dv = bump("dve")
                emit("vector", lambda e: e.scalar_tensor_tensor(
                    st[:, 3:4], st[:, 1:2], 1.0 / dcols, st[:, 3:4], ALU.mult, ALU.add).then_inc(sems["dve"], 1))
                wait("scalar", "dve", dv)
                av = bump("act")
                emit("scalar", lambda e: e.activation(st[:, 4:5], st[:, 3:4], AF.Ln, bias=EPS).then_inc(sems["act"], 1))
                wait("scalar", "act", av)
                av = bump("act")
                emit("scalar", lambda e: e.activation(st[:, 3:4], st[:, 4:5], AF.Exp, scale=-0.5).then_inc(sems["act"], 1))
                wait("vector", "act", av)
                dv = bump("dve")
                emit("vector", lambda e, s=src_ap, d=dst_ap: e.tensor_scalar(
                    d, s, st[:, 2:3], st[:, 3:4], ALU.subtract, ALU.mult).then_inc(sems["dve"], 1))
                return dv

            for tb in range(NB):
                def ln2_stats(ts):
                    sl = ts % 2
                    gt = tb * NTS + ts
                    wait("sync", "x1st0", X1ST_ALL[0])
                    wait("sync", "x1st1", X1ST_ALL[1])
                    wait("sync", "dve", cnt.get(f"x1i_free{sl}", 0))
                    xv = bump(f"xb{sl}", 16)
                    emit("sync", lambda e, g=gt, s=sl: e.dma_start(
                        s_x1i[s][:], d_x1[g * 128:(g + 1) * 128, :]).then_inc(sems[f"xb{s}"], 16))
                    wait("scalar", f"xb{sl}", xv)
                    wait("vector", f"xb{sl}", xv)
                    wait("vector", "pe", cnt.get(f"xn2_free{sl}", 0))
                    dv = ln_norm(s_x1i[sl][:], s_xn2_[sl][:], D, s_sq2_[sl],
                                 s_st2[:, sl * 16: sl * 16 + 16])
                    cnt[f"x1i_free{sl}"] = dv
                    cnt[f"ln2s_{ts}"] = dv

                def ln2_finish(ts):
                    sl = ts % 2
                    wait("tensor", "dve", cnt[f"ln2s_{ts}"])
                    for ic in range(DC):
                        wait("tensor", "dve", cnt.get("ln2_evac", 0))
                        wait("tensor", "act", cnt.get("pg_a0", 0))
                        wait("tensor", "dve", cnt.get("pg_f0", 0))
                        pv = bump("pe")
                        emit("tensor", lambda e, i=ic, s=sl: e.transpose(
                            p_g[:, 0:128], s_xn2_[s][:, i * 128:(i + 1) * 128],
                            c_ident[:]).then_inc(sems["pe"], 1))
                        wait("vector", "pe", pv)
                        cnt["ln2_evac"] = bump("dve")
                        emit("vector", lambda e, i=ic, t=ts: e.tensor_copy(
                            s_ln2T[:, i * TB + t * 128: i * TB + (t + 1) * 128],
                            p_g[:, 0:128]).then_inc(sems["dve"], 1))
                    cnt[f"xn2_free{sl}"] = cnt["pe"]
                    cnt["pg_f0"] = cnt["ln2_evac"]

                def ln2_router(ts):
                    sl = ts % 2
                    wait("tensor", "dve", cnt["ln2_evac"])
                    wait("tensor", "dve", cnt.get("pd_f0", 0))
                    for ic in range(DC):
                        pv = bump("pe")
                        emit("tensor", lambda e, i=ic, t=ts: e.matmul(
                            p_d[:, 0:E], s_ln2T[:, i * TB + t * 128: i * TB + (t + 1) * 128],
                            s_wrT[:, i * E:(i + 1) * E],
                            start=(i == 0), stop=(i == DC - 1)).then_inc(sems["pe"], 1))
                    RLG = cnt["pe"]
                    wait("vector", "pe", RLG)
                    wait("vector", "dve", cnt["dve"])
                    dv = bump("dve")
                    emit("vector", lambda e: e.tensor_tensor(
                        s_sm[:, 0:E], p_d[:, 0:E], s_brb[:, 0:E], ALU.add).then_inc(sems["dve"], 1))
                    cnt["pd_f0"] = dv
                    wait("vector", "dve", dv)
                    dv = bump("dve")
                    emit("vector", lambda e: e.reduce_max(
                        s_sm[:, E:E + 1], s_sm[:, 0:E], axis=AX, negate=True).then_inc(sems["dve"], 1))
                    wait("scalar", "dve", dv)
                    av = bump("act")
                    emit("scalar", lambda e: e.activation(
                        s_sm[:, E + 2:2 * E + 2], s_sm[:, 0:E], AF.Exp,
                        bias=s_sm[:, E:E + 1], accum_out=s_sm[:, E + 1:E + 2]).then_inc(sems["act"], 1))
                    wait("vector", "act", av)
                    dv = bump("dve")
                    emit("vector", lambda e: e.reciprocal(
                        s_sm[:, 2 * E + 2:2 * E + 3], s_sm[:, E + 1:E + 2]).then_inc(sems["dve"], 1))
                    wait("vector", "dve", dv)
                    dv = bump("dve")
                    emit("vector", lambda e: e.tensor_scalar(
                        s_sm[:, 0:E], s_sm[:, E + 2:2 * E + 2], s_sm[:, 2 * E + 2:2 * E + 3],
                        None, ALU.mult).then_inc(sems["dve"], 1))
                    wait("vector", "dve", dv)
                    dv = bump("dve")
                    emit("vector", lambda e: e.reduce_max(
                        s_sm[:, E:E + 1], s_sm[:, 0:E], axis=AX).then_inc(sems["dve"], 1))
                    wait("vector", "dve", dv)
                    dv = bump("dve")
                    emit("vector", lambda e: e.tensor_scalar(
                        s_sm[:, E + 2:2 * E + 2], s_sm[:, 0:E], s_sm[:, E:E + 1],
                        None, ALU.is_equal).then_inc(sems["dve"], 1))
                    wait("vector", "dve", dv)
                    dv = bump("dve")
                    emit("vector", lambda e: e.scalar_tensor_tensor(
                        s_sm[:, E + 2:2 * E + 2], s_sm[:, E + 2:2 * E + 2], -1e30,
                        s_sm[:, 0:E], ALU.mult, ALU.add).then_inc(sems["dve"], 1))
                    wait("vector", "dve", dv)
                    dv = bump("dve")
                    emit("vector", lambda e: e.reduce_max(
                        s_sm[:, E:E + 1], s_sm[:, E + 2:2 * E + 2], axis=AX).then_inc(sems["dve"], 1))
                    wait("vector", "dve", dv)
                    dv = bump("dve")
                    emit("vector", lambda e: e.tensor_scalar(
                        s_sm[:, E + 2:2 * E + 2], s_sm[:, 0:E], s_sm[:, E:E + 1],
                        None, ALU.is_ge).then_inc(sems["dve"], 1))
                    wait("vector", "dve", dv)
                    dv = bump("dve")
                    emit("vector", lambda e, t=ts: e.tensor_tensor(
                        s_w8[:, t * E:(t + 1) * E], s_sm[:, 0:E], s_sm[:, E + 2:2 * E + 2],
                        ALU.mult).then_inc(sems["dve"], 1))
                    wait("tensor", "dve", dv)
                    wait("tensor", "dve", cnt.get("pg_f0", 0))
                    pv = bump("pe")
                    emit("tensor", lambda e, t=ts: e.transpose(
                        p_g[0:E, 0:128], s_w8[:, t * E:(t + 1) * E], c_ident[:]).then_inc(sems["pe"], 1))
                    wait("vector", "pe", pv)
                    dv = bump("dve")
                    emit("vector", lambda e, t=ts: e.tensor_copy(
                        s_wt8[0:E, t * 128:(t + 1) * 128], p_g[0:E, 0:128]).then_inc(sems["dve"], 1))
                    cnt["pg_f0"] = dv
                    wait("tensor", "dve", dv)
                    wait("tensor", "dve", cnt.get("pd_f1", 0))
                    wait("vector", "dve", cnt.get(f"acc_used_{ts}", 0))
                    for n0 in range(0, D, 512):
                        w5 = min(512, D - n0)
                        pv = bump("pe")
                        emit("tensor", lambda e, t=ts, n=n0, w=w5: e.matmul(
                            p_d[:, PDW + n: PDW + n + w],
                            s_wt8[0:E, t * 128:(t + 1) * 128],
                            s_dbb[0:E, n:n + w],
                            start=True, stop=True).then_inc(sems["pe"], 1))
                    wait("vector", "pe", cnt["pe"])
                    dv = bump("dve")
                    emit("vector", lambda e, t=ts: e.tensor_copy(
                        s_acc[:, t * D:(t + 1) * D], p_d[:, PDW:PDW + D]).then_inc(sems["dve"], 1))
                    cnt[f"acc_v_{ts}"] = dv
                    cnt["pd_f1"] = dv

                for ts in range(NTS):
                    ln2_stats(ts)
                    ln2_finish(ts)
                    ln2_router(ts)

                def ln3_fin(ts):
                    sl = ts % 2
                    gt = tb * NTS + ts
                    wait("sync", "dve", cnt.get(f"x1i_free{sl}", 0))
                    xv = bump(f"xb{sl}", 16)
                    emit("sync", lambda e, g=gt, s=sl: e.dma_start(
                        s_x1i[s][:], d_x1[g * 128:(g + 1) * 128, :]).then_inc(sems[f"xb{s}"], 16))
                    wait("vector", "dve", cnt[f"acc_v_{ts}"])
                    wait("scalar", "dve", cnt[f"acc_v_{ts}"])
                    wait("vector", f"yst{sl}", cnt.get(f"out_free{sl}", 0))
                    dv = ln_norm(s_acc[:, ts * D:(ts + 1) * D], s_out[sl][:], D,
                                 s_sq2_[sl], s_st2[:, sl * 16: sl * 16 + 16])
                    cnt[f"acc_used_{ts}"] = dv
                    wait("vector", "dve", dv)
                    wait("vector", f"xb{sl}", xv)
                    dv = bump("dve")
                    emit("vector", lambda e, s=sl: e.tensor_tensor(
                        s_out[s][:], s_out[s][:], s_x1i[s][:], ALU.add).then_inc(sems["dve"], 1))
                    cnt[f"x1i_free{sl}"] = dv
                    wait("scalar", "dve", dv)
                    yv = bump(f"yst{sl}", 16)
                    emit("scalar", lambda e, g=gt, s=sl: e.dma_start(
                        d_y[g * 128:(g + 1) * 128, :], s_out[s][:]).then_inc(sems[f"yst{s}"], 16))
                    cnt[f"out_free{sl}"] = yv

                ln3_done = set()

                for ei in range(E if ("D2" in PHASES or PHASES == "ABCD") else 0):
                    wait("sync", "pe", cnt.get("wdn_free", 0))
                    wv = bump("wd0", 16)
                    emit("sync", lambda e, ei=ei: e.dma_start(
                        s_wd[:].rearrange("p (c h) -> p c h", c=NIC),
                        d_wdb[ei].rearrange("(c p) h -> p c h", p=128)).then_inc(sems["wd0"], 16))
                    cnt["wdn_last"] = wv

                    for sl_i in range(NSL):
                        buf = (ei * NSL + sl_i) % 4
                        sname = f"wg{buf}"
                        wait("sync", "pe", cnt.get(f"wg_free{buf}", 0))
                        wv1 = bump(sname, 16)
                        emit("sync", lambda e, ei=ei, s=sl_i, b=buf, sm=sems[sname]: e.dma_start(
                            s_wg[b][:, 0:DC * 256].rearrange("p (c n) -> p c n", n=256),
                            d_wgb[ei, :, s * 256:(s + 1) * 256].rearrange("(c p) n -> p c n", p=128)
                        ).then_inc(sm, 16))
                        wv2 = bump(sname, 16)
                        emit("sync", lambda e, ei=ei, s=sl_i, b=buf, sm=sems[sname]: e.dma_start(
                            s_wg[b][:, DC * 256:DC * 512].rearrange("p (c n) -> p c n", n=256),
                            d_wgb[ei, :, I + s * 256: I + (s + 1) * 256].rearrange("(c p) n -> p c n", p=128)
                        ).then_inc(sm, 16))

                        wait("tensor", sname, wv2)
                        for grp in range(4):
                            isup = grp >= 2
                            nn = grp % 2
                            nchunk = sl_i * 2 + nn
                            bcol = ei * NCH + (NIC + nchunk if isup else nchunk)
                            col0 = (DC * 256 if isup else 0) + nn * 128
                            for t5 in range(TT5):
                                bi = cnt.get("gu_rot", 0) % 4
                                cnt["gu_rot"] = cnt.get("gu_rot", 0) + 1
                                wait("tensor", "dve", cnt.get(f"pg_f{bi}", 0))
                                wait("tensor", "act", cnt.get(f"pg_a{bi}", 0))
                                for ic in range(DC):
                                    pv = bump("pe")
                                    emit("tensor", lambda e, b=buf, i=ic, t=t5, c0=col0, bi=bi: e.matmul(
                                        p_g[:, bi * 512:(bi + 1) * 512],
                                        s_wg[b][:, c0 + i * 256: c0 + i * 256 + 128],
                                        s_ln2T[:, i * TB + t * 512: i * TB + (t + 1) * 512],
                                        start=(i == 0), stop=(i == DC - 1)).then_inc(sems["pe"], 1))
                                GUP = cnt["pe"]
                                if "e" not in EXPERT_PARTS:
                                    continue
                                if not isup:
                                    wait("scalar", "pe", GUP)
                                    wait("scalar", "dve", W_PREP_DVE)
                                    wait("scalar", "pool", cnt.get(f"sig_used_{t5}", 0))
                                    av = bump("act")
                                    emit("scalar", lambda e, bi=bi, bc=bcol, t=t5: e.activation(
                                        s_sig[:, t * 512:(t + 1) * 512],
                                        p_g[:, bi * 512:(bi + 1) * 512], AF.Sigmoid,
                                        scale=ALPHA, bias=s_gubs[:, bc:bc + 1]).then_inc(sems["act"], 1))
                                    cnt[f"pg_a{bi}"] = av
                                    wait("vector", "pe", GUP)
                                    wait("vector", "act", av)
                                    wait("vector", "pool", cnt.get(f"uc_used_{t5}", 0))
                                    dv = bump("dve")
                                    emit("vector", lambda e, bi=bi, bc=bcol, t=t5: e.tensor_scalar(
                                        s_uc[:, t * 512:(t + 1) * 512],
                                        p_g[:, bi * 512:(bi + 1) * 512],
                                        s_gub[:, bc:bc + 1], LIMIT, ALU.add, ALU.min).then_inc(sems["dve"], 1))
                                    cnt[f"pg_f{bi}"] = dv
                                    wait("gpsimd", "dve", dv)
                                    wait("gpsimd", "act", av)
                                    wait("gpsimd", "pool", cnt.get(f"glu_used_{nn}_{t5}", 0))
                                    lv = bump("pool")
                                    emit("gpsimd", lambda e, n=nn, t=t5: e.tensor_tensor(
                                        s_glu[n][:, t * 512:(t + 1) * 512],
                                        s_uc[:, t * 512:(t + 1) * 512],
                                        s_sig[:, t * 512:(t + 1) * 512],
                                        ALU.mult).then_inc(sems["pool"], 1))
                                    cnt[f"glu_{nn}_{t5}"] = lv
                                    cnt[f"uc_used_{t5}"] = lv
                                    cnt[f"sig_used_{t5}"] = lv
                                else:
                                    wait("vector", "pe", GUP)
                                    wait("vector", "pool", cnt.get(f"uc_used_{t5}", 0))
                                    dv = bump("dve")
                                    emit("vector", lambda e, bi=bi, bc=bcol, t=t5: e.tensor_scalar(
                                        s_uc[:, t * 512:(t + 1) * 512],
                                        p_g[:, bi * 512:(bi + 1) * 512],
                                        s_gubp[:, bc:bc + 1], LIMIT + 1.0, ALU.add, ALU.min).then_inc(sems["dve"], 1))
                                    cnt[f"pg_f{bi}"] = dv
                                    cnt[f"pg_a{bi}"] = cnt.get("act", 0)
                                    wait("vector", "dve", dv)
                                    dv = bump("dve")
                                    emit("vector", lambda e, t=t5: e.tensor_scalar(
                                        s_uc[:, t * 512:(t + 1) * 512],
                                        s_uc[:, t * 512:(t + 1) * 512], -(LIMIT - 1.0), None,
                                        ALU.max).then_inc(sems["dve"], 1))
                                    wait("gpsimd", "dve", dv)
                                    wait("gpsimd", "pool", cnt[f"glu_{nn}_{t5}"])
                                    wait("gpsimd", "pe", cnt.get("wdn_free", 0))
                                    lv = bump("pool")
                                    emit("gpsimd", lambda e, n=nn, na=nchunk, t=t5: e.tensor_tensor(
                                        s_act[:, na * TB + t * 512: na * TB + (t + 1) * 512],
                                        s_uc[:, t * 512:(t + 1) * 512],
                                        s_glu[n][:, t * 512:(t + 1) * 512],
                                        ALU.mult).then_inc(sems["pool"], 1))
                                    cnt[f"uc_used_{t5}"] = lv
                                    cnt[f"glu_used_{nn}_{t5}"] = lv
                                    cnt[f"actc_{nchunk}"] = lv
                        cnt[f"wg_free{buf}"] = cnt["pe"]

                    wait("tensor", "wd0", cnt["wdn_last"])
                    for ts in range(NTS if "d" in EXPERT_PARTS else 0):
                        dsel = ts % 2
                        wait("tensor", "dve", cnt.get(f"pd_f{dsel}", 0))
                        for ncx in range(NIC):
                            wait("tensor", "pool", cnt.get(f"actc_{ncx}", 0))
                            for n0 in range(0, D, 512):
                                w5 = min(512, D - n0)
                                pv = bump("pe")
                                emit("tensor", lambda e, c=ncx, t=ts, n=n0, w=w5, d=dsel: e.matmul(
                                    p_d[:, d * PDW + n: d * PDW + n + w],
                                    s_act[:, c * TB + t * 128: c * TB + (t + 1) * 128],
                                    s_wd[:, c * D + n: c * D + n + w],
                                    start=(c == 0), stop=(c == NIC - 1)).then_inc(sems["pe"], 1))
                        DNP = cnt["pe"]
                        wait("vector", "pe", DNP)
                        wait("vector", "dve", cnt[f"acc_v_{ts}"])
                        dv = bump("dve")
                        emit("vector", lambda e, t=ts, d=dsel, ei=ei: e.scalar_tensor_tensor(
                            s_acc[:, t * D:(t + 1) * D], p_d[:, d * PDW: d * PDW + D],
                            s_w8[:, t * E + ei: t * E + ei + 1], s_acc[:, t * D:(t + 1) * D],
                            ALU.mult, ALU.add).then_inc(sems["dve"], 1))
                        cnt[f"acc_v_{ts}"] = dv
                        cnt[f"pd_f{dsel}"] = dv
                        if False and ei == E - 1 and PHASES in ("ABCD", "ABCD3"):
                            ln3_fin(ts)
                            ln3_done.add(ts)
                    cnt["wdn_free"] = cnt["pe"]

                for ts in range(NTS if PHASES in ("ABCD", "ABCD3") else 0):
                    if ts not in ln3_done:
                        ln3_fin(ts)
                ln3_done.clear()

            if PHASES != "ABCD":
                wait("scalar", "dve", cnt["dve"])
                wait("scalar", "act", cnt["act"])
                wait("scalar", "pool", cnt.get("pool", 0))
                yv = bump("yst0", 16)
                emit("scalar", lambda e: e.dma_start(
                    d_y[0:128, :], s_x1i[0][:]).then_inc(sems["yst0"], 16))
            wait("sync", "yst0", cnt.get("yst0", 0))
            wait("sync", "yst1", cnt.get("yst1", 0))
            for s in ("wd0", "wg0", "wg1", "wg2", "wg3"):
                wait("sync", s, cnt.get(s, 0))
            run_block()

    return nc


def _consts():
    ident = np.eye(128, dtype=np.float32)
    iu = np.triu(np.ones((128, 128), dtype=np.float32), 1)
    return dict(
        c_ident=ident,
        c_masks=np.ascontiguousarray(iu * NEG),
        c_maskt=np.ascontiguousarray(iu.T * NEG),
        c_ones=np.ones((1, 128), dtype=np.float32),
        c_onesb=np.ones((1, 128), dtype=ml_dtypes.bfloat16),
        c_eps=np.full((128, 1), EPS, dtype=np.float32),
        c_sel=np.concatenate([
            np.concatenate([np.ones((1, 64)), np.zeros((1, 64))], 1),
            np.concatenate([np.zeros((1, 64)), np.ones((1, 64))], 1)], 0).astype(ml_dtypes.bfloat16),
    )


def kernel(L0_attn_x, L0_attn_Wq, L0_attn_Wk, L0_attn_Wv, L0_attn_Wo,
           L0_ffn_router_weight, L0_ffn_router_bias, L0_ffn_gate_up_proj,
           L0_ffn_gate_up_bias, L0_ffn_down_proj, L0_ffn_down_bias):
    cfg = CFG()
    x = np.ascontiguousarray(np.asarray(L0_attn_x, dtype=np.float32))
    B = x.shape[0]
    base = dict(
        wq=np.ascontiguousarray(L0_attn_Wq, dtype=np.float32),
        wk=np.ascontiguousarray(L0_attn_Wk, dtype=np.float32),
        wv=np.ascontiguousarray(L0_attn_Wv, dtype=np.float32),
        wo=np.ascontiguousarray(L0_attn_Wo, dtype=np.float32),
        wr=np.ascontiguousarray(L0_ffn_router_weight, dtype=np.float32),
        br=np.ascontiguousarray(np.asarray(L0_ffn_router_bias, dtype=np.float32).reshape(1, -1)),
        wg=np.ascontiguousarray(L0_ffn_gate_up_proj, dtype=np.float32),
        bg=np.ascontiguousarray(L0_ffn_gate_up_bias, dtype=np.float32),
        wd=np.ascontiguousarray(L0_ffn_down_proj, dtype=np.float32),
        bd=np.ascontiguousarray(L0_ffn_down_bias, dtype=np.float32),
    )
    base.update(_consts())
    nc = build_nc(cfg)
    in_maps = [dict(base, x=np.ascontiguousarray(x[c])) for c in range(B)]
    res = run_bass_kernel_spmd(nc, in_maps, core_ids=list(range(B)))
    return np.stack([np.asarray(res.results[c]["y"]) for c in range(B)], axis=0)


# version 26 (final)
# speedup vs baseline: 1.1390x; 1.1390x over previous
import contextlib
from dataclasses import dataclass

import numpy as np
import ml_dtypes

import concourse.bass as bass
import concourse.mybir as mybir
from concourse.bass_utils import run_bass_kernel_spmd

F32 = mybir.dt.float32
BF16 = mybir.dt.bfloat16
AX = mybir.AxisListType.X
ALU = mybir.AluOpType
AF = mybir.ActivationFunctionType

ENGINES = ("tensor", "vector", "scalar", "gpsimd", "sync")
PHASES = "ABCD"
A_PARTS = "wrgl"
EXPERT_PARTS = "ged"

NEG = -1.0e9
EPS = 1.0e-5
ALPHA = 1.702
LIMIT = 7.0


@dataclass
class CFG:
    T: int = 4096
    D: int = 1024
    H: int = 16
    E: int = 8
    I: int = 2048
    TB: int = 1024
    QG: int = 4

    @property
    def NT(self): return self.T // 128
    @property
    def DC(self): return self.D // 128
    @property
    def NP(self): return self.H // 2
    @property
    def NB(self): return self.T // self.TB
    @property
    def NTS(self): return self.TB // 128
    @property
    def NIC(self): return self.I // 128
    @property
    def NSL(self): return self.I // 256
    @property
    def NQG(self): return self.NT // self.QG


def build_nc(cfg: CFG):
    nc = bass.Bass("TRN2", target_bir_lowering=False)
    T, D, H, E, I, TB = cfg.T, cfg.D, cfg.H, cfg.E, cfg.I, cfg.TB
    NT, DC, NP, NB, NTS = cfg.NT, cfg.DC, cfg.NP, cfg.NB, cfg.NTS
    NIC, NSL, QG, NQG = cfg.NIC, cfg.NSL, cfg.QG, cfg.NQG
    I2 = 2 * I
    NCH = I2 // 128
    TT5 = TB // 512
    PDW = max(D, 512)
    PGW = max(TB, 512)

    d_x = nc.dram_tensor("x", [T, D], F32, kind="ExternalInput").ap()
    d_wq = nc.dram_tensor("wq", [D, D], F32, kind="ExternalInput").ap()
    d_wk = nc.dram_tensor("wk", [D, D], F32, kind="ExternalInput").ap()
    d_wv = nc.dram_tensor("wv", [D, D], F32, kind="ExternalInput").ap()
    d_wo = nc.dram_tensor("wo", [D, D], F32, kind="ExternalInput").ap()
    d_wr = nc.dram_tensor("wr", [E, D], F32, kind="ExternalInput").ap()
    d_br = nc.dram_tensor("br", [1, E], F32, kind="ExternalInput").ap()
    d_wg = nc.dram_tensor("wg", [E, D, I2], F32, kind="ExternalInput").ap()
    d_bg = nc.dram_tensor("bg", [E, I2], F32, kind="ExternalInput").ap()
    d_wd = nc.dram_tensor("wd", [E, I, D], F32, kind="ExternalInput").ap()
    d_bd = nc.dram_tensor("bd", [E, D], F32, kind="ExternalInput").ap()
    d_ident = nc.dram_tensor("c_ident", [128, 128], F32, kind="ExternalInput").ap()
    d_masks = nc.dram_tensor("c_masks", [128, 128], F32, kind="ExternalInput").ap()
    d_maskt = nc.dram_tensor("c_maskt", [128, 128], F32, kind="ExternalInput").ap()
    d_ones = nc.dram_tensor("c_ones", [1, 128], F32, kind="ExternalInput").ap()
    d_onesb = nc.dram_tensor("c_onesb", [1, 128], BF16, kind="ExternalInput").ap()
    d_eps = nc.dram_tensor("c_eps", [128, 1], F32, kind="ExternalInput").ap()
    d_sel = nc.dram_tensor("c_sel", [2, 128], BF16, kind="ExternalInput").ap()
    d_y = nc.dram_tensor("y", [T, D], F32, kind="ExternalOutput").ap()
    d_oscr = nc.dram_tensor("o_scr", [NP * 128, T], BF16).ap()
    d_x1 = nc.dram_tensor("x1_scr", [T, D], F32).ap()
    d_wgb = nc.dram_tensor("wg_bf", [E, D, I2], BF16).ap()
    d_woscr = nc.dram_tensor("woT_scr", [128, 8 * 1024], BF16).ap()
    d_wdb = nc.dram_tensor("wd_bf", [E, I, D], BF16).ap()

    prog = {e: [] for e in ENGINES}
    cnt = {}
    waited = {e: {} for e in ENGINES}
    sems = {}

    def emit(eng, fn):
        prog[eng].append(fn)

    def bump(name, n=1):
        cnt[name] = cnt.get(name, 0) + n
        return cnt[name]

    def wait(eng, name, val=None):
        if val is None:
            val = cnt.get(name, 0)
        assert val <= cnt.get(name, 0), (eng, name, val, cnt.get(name, 0))
        if waited[eng].get(name, -1) >= val:
            return
        waited[eng][name] = val
        emit(eng, lambda e, s=sems[name], v=val: e.wait_ge(s, v))

    def run_block():
        with nc.Block() as blk:
            for eng in ENGINES:
                lst = prog[eng]
                prog[eng] = []
                if not lst:
                    continue
                def body(e, lst=lst):
                    with nc.allow_low_precision(reason="deliberate bf16 operands; fp32 accumulation"):
                        for f in lst:
                            f(e)
                getattr(blk, eng)(body)

    ctx = contextlib.ExitStack()
    EN = ctx.enter_context
    with ctx:
        for s in ("pe", "dve", "act", "pool", "ldc",
                  "xa0", "xa1", "ob0", "ob1", "xb0", "xb1",
                  "wg0", "wg1", "wg2", "wg3", "wd0", "wd1", "wpre", "wrl", "wcast",
                  "ost", "x1st0", "x1st1", "yst0", "yst1"):
            sems[s] = EN(nc.semaphore(s))

        c_ident = EN(nc.sbuf_tensor("c_ident_s", [128, 128], F32))
        c_masks = EN(nc.sbuf_tensor("c_masks_s", [128, 128], F32))
        c_maskt = EN(nc.sbuf_tensor("c_maskt_s", [128, 128], F32))
        c_ones = EN(nc.sbuf_tensor("c_ones_s", [1, 128], F32))
        c_onesb = EN(nc.sbuf_tensor("c_onesb_s", [1, 128], BF16))
        s_brb = EN(nc.sbuf_tensor("s_brb", [128, E], F32))
        s_db = EN(nc.sbuf_tensor("s_db", [E, D], F32))
        s_dbb = EN(nc.sbuf_tensor("s_dbb", [E, D], BF16))
        s_wrT = EN(nc.sbuf_tensor("s_wrT", [128, DC * E], BF16))
        s_gub = EN(nc.sbuf_tensor("s_gub", [128, E * NCH], F32))
        s_gubs = EN(nc.sbuf_tensor("s_gubs", [128, E * NCH], F32))
        s_gubp = EN(nc.sbuf_tensor("s_gubp", [128, E * NCH], F32))
        s_br_row = EN(nc.sbuf_tensor("s_br_row", [1, E], F32))
        c_eps = EN(nc.sbuf_tensor("c_eps_s", [128, 1], F32))
        c_sel = EN(nc.sbuf_tensor("c_sel_s", [2, 128], BF16))
        nc.const_aps.register(F32, EPS, c_eps[:])

        for dst, src in ((c_ident, d_ident), (c_masks, d_masks), (c_maskt, d_maskt),
                         (c_ones, d_ones), (c_onesb, d_onesb), (s_db, d_bd),
                         (s_br_row, d_br), (c_eps, d_eps), (c_sel, d_sel)):
            bump("ldc", 16)
            emit("sync", lambda e, a=dst, b=src: e.dma_start(a[:], b).then_inc(sems["ldc"], 16))
        LDC_ALL = cnt["ldc"]
        for _eng in ENGINES:
            wait(_eng, "ldc", LDC_ALL)

        with contextlib.ExitStack() as ab:
            EA = ab.enter_context
            s_ln1T = EA(nc.sbuf_tensor("s_ln1T", [128, DC * T], BF16))
            s_woT = EA(nc.sbuf_tensor("s_woT", [128, DC * D], BF16))
            s_wqT = EA(nc.sbuf_tensor("s_wqT", [128, DC * D], BF16))
            s_wkT = EA(nc.sbuf_tensor("s_wkT", [128, DC * D], BF16))
            s_wvT = EA(nc.sbuf_tensor("s_wvT", [128, DC * D], BF16))

            with contextlib.ExitStack() as aw:
                EW = aw.enter_context
                s_wst = EW(nc.sbuf_tensor("s_wst", [128, DC * D], F32))
                s_gb_st = EW(nc.sbuf_tensor("s_gb_st", [1, I2], F32))
                s_wr_st = EW(nc.sbuf_tensor("s_wr_st", [E, D], BF16))
                s_identb = EW(nc.sbuf_tensor("s_identb", [128, 128], BF16))
                p_tr = EW(nc.psum_tensor("p_trA", [128, 1024], F32))
                p_trb = EW(nc.psum_tensor("p_trb", [128, 128], BF16))

                wait("vector", "ldc", LDC_ALL)
                dv = bump("dve")
                emit("vector", lambda e: e.tensor_copy(s_identb[:], c_ident[:]).then_inc(sems["dve"], 1))
                dv = bump("dve")
                emit("vector", lambda e: e.tensor_copy(s_dbb[:], s_db[:]).then_inc(sems["dve"], 1))

                def prep_w(dsrc, dstT, tag):
                    v = bump("wpre", 16)
                    wait("sync", "pe", cnt.get("pe", 0))
                    emit("sync", lambda e, s=dsrc: e.dma_start(
                        s_wst[:].rearrange("p (c i) -> p c i", c=DC),
                        s.rearrange("(c p) i -> p c i", p=128)).then_inc(sems["wpre"], 16))
                    wait("tensor", "wpre", v)
                    wait("tensor", "ldc", LDC_ALL)
                    for oc in range(DC):
                        for ic in range(DC):
                            half = (oc * DC + ic) % 2
                            wait("tensor", "dve", cnt.get(f"wpA_h{half}", 0))
                            pv = bump("pe")
                            emit("tensor", lambda e, o=oc, i=ic, h=half: e.transpose(
                                p_tr[:, h * 512: h * 512 + 128],
                                s_wst[:, o * D + i * 128: o * D + (i + 1) * 128],
                                c_ident[:]).then_inc(sems["pe"], 1))
                            wait("vector", "pe", pv)
                            dv = bump("dve")
                            emit("vector", lambda e, o=oc, i=ic, h=half, d=dstT: e.tensor_copy(
                                d[:, i * D + o * 128: i * D + (o + 1) * 128],
                                p_tr[:, h * 512: h * 512 + 128]).then_inc(sems["dve"], 1))
                            cnt[f"wpA_h{half}"] = dv

                for dsrc, dstT, tag in (((d_wq, s_wqT, "q"), (d_wk, s_wkT, "k"),
                                         (d_wv, s_wvT, "v"), (d_wo, s_woT, "o"))
                                        if A_PARTS and "w" in A_PARTS else ()):
                    prep_w(dsrc, dstT, tag)

                if "r" in A_PARTS:
                    v = bump("wrl", 16)
                    emit("gpsimd", lambda e: e.dma_start(s_wr_st[:], d_wr).then_inc(sems["wrl"], 16))
                    wait("tensor", "wrl", v)
                    wait("tensor", "dve", cnt["dve"])
                    for ic in range(DC):
                        wait("tensor", "dve", cnt.get("wr_evac", 0))
                        pv = bump("pe")
                        emit("tensor", lambda e, i=ic: e.transpose(
                            p_trb[:, 0:E], s_wr_st[:, i * 128:(i + 1) * 128],
                            s_identb[0:E, 0:E]).then_inc(sems["pe"], 1))
                        wait("vector", "pe", pv)
                        cnt["wr_evac"] = bump("dve")
                        emit("vector", lambda e, i=ic: e.tensor_copy(
                            s_wrT[:, i * E:(i + 1) * E], p_trb[:, 0:E]).then_inc(sems["dve"], 1))

                    wait("tensor", "dve", cnt["dve"])
                    pv = bump("pe")
                    emit("tensor", lambda e: e.matmul(p_tr[:, 0:E], c_ones[:], s_br_row[:],
                                                      start=True, stop=True).then_inc(sems["pe"], 1))
                    wait("vector", "pe", pv)
                    dv = bump("dve")
                    emit("vector", lambda e: e.tensor_copy(s_brb[:], p_tr[:, 0:E]).then_inc(sems["dve"], 1))

                    wait("tensor", "dve", cnt["dve"])
                    for ei in range(E):
                        v = bump("wpre", 16)
                        wait("sync", "pe", cnt.get("pe", 0))
                        emit("sync", lambda e, ei=ei: e.dma_start(
                            s_gb_st[:], d_bg[ei:ei + 1, :]).then_inc(sems["wpre"], 16))
                        wait("tensor", "wpre", v)
                        for ch in range(NCH):
                            wait("tensor", "dve", cnt.get("gb_evac", 0))
                            pv = bump("pe")
                            emit("tensor", lambda e, c=ch: e.transpose(
                                p_tr[:, 0:1], s_gb_st[:, c * 128:(c + 1) * 128],
                                c_ident[0:1, 0:1]).then_inc(sems["pe"], 1))
                            wait("vector", "pe", pv)
                            cnt["gb_evac"] = bump("dve")
                            emit("vector", lambda e, ei=ei, c=ch: e.tensor_copy(
                                s_gub[:, ei * NCH + c: ei * NCH + c + 1], p_tr[:, 0:1]).then_inc(sems["dve"], 1))
                    wait("vector", "dve", cnt["dve"])
                    dv = bump("dve")
                    emit("vector", lambda e: e.tensor_scalar(s_gubs[:], s_gub[:], ALPHA, None,
                                                             ALU.mult).then_inc(sems["dve"], 1))
                    dv = bump("dve")
                    emit("vector", lambda e: e.tensor_scalar(s_gubp[:], s_gub[:], 1.0, None,
                                                             ALU.add).then_inc(sems["dve"], 1))
                W_PREP_DVE = cnt["dve"]

            ln1_ctx = contextlib.ExitStack()
            E1 = ln1_ctx.enter_context
            s_xa = [E1(nc.sbuf_tensor(f"s_xa{i}", [128, D], F32, side="right")) for i in range(2)]
            s_xn = [E1(nc.sbuf_tensor(f"s_xn{i}", [128, D], F32, side="right")) for i in range(2)]
            s_sq = E1(nc.sbuf_tensor("s_sq", [128, D], BF16, side="right"))
            s_st = E1(nc.sbuf_tensor("s_st", [128, 16], F32, side="right"))
            p_ln = E1(nc.psum_tensor("p_ln", [128, 1024], F32))

            def ln_stats(src_ap, dcols, sq, st):
                wait("scalar", "dve", cnt["dve"])
                wait("vector", "act", cnt["act"] if "act" in cnt else 0)
                av = bump("act")
                emit("scalar", lambda e, s=src_ap: e.activation(
                    sq[:, 0:dcols], s, AF.Square, accum_out=st[:, 1:2]).then_inc(sems["act"], 1))
                dv = bump("dve")
                emit("vector", lambda e, s=src_ap: e.reduce_sum(
                    st[:, 0:1], s, axis=AX).then_inc(sems["dve"], 1))
                wait("vector", "dve", dv)
                wait("vector", "act", av)
                dv = bump("dve")
                emit("vector", lambda e: e.tensor_scalar(st[:, 2:3], st[:, 0:1], 1.0 / dcols,
                                                         None, ALU.mult).then_inc(sems["dve"], 1))
                wait("vector", "dve", dv)
                dv = bump("dve")
                emit("vector", lambda e: e.scalar_tensor_tensor(
                    st[:, 3:4], st[:, 2:3], -1.0, st[:, 2:3], ALU.mult, ALU.mult).then_inc(sems["dve"], 1))
                wait("vector", "dve", dv)
                dv = bump("dve")
                emit("vector", lambda e: e.scalar_tensor_tensor(
                    st[:, 3:4], st[:, 1:2], 1.0 / dcols, st[:, 3:4], ALU.mult, ALU.add).then_inc(sems["dve"], 1))
                wait("scalar", "dve", dv)
                av = bump("act")
                emit("scalar", lambda e: e.activation(st[:, 4:5], st[:, 3:4], AF.Ln,
                                                      bias=EPS).then_inc(sems["act"], 1))
                wait("scalar", "act", av)
                av = bump("act")
                emit("scalar", lambda e: e.activation(st[:, 3:4], st[:, 4:5], AF.Exp,
                                                      scale=-0.5).then_inc(sems["act"], 1))
                return av

            wait("sync", "pe", cnt["pe"])
            wait("sync", "dve", cnt["dve"])
            for _eng in ("vector", "scalar", "sync"):
                wait(_eng, "wpre", cnt.get("wpre", 0))
            for tt in range(NT if "l" in A_PARTS else 0):
                sl = tt % 2
                wait("sync", "dve", cnt.get(f"xa_free{sl}", 0))
                xv = bump(f"xa{sl}", 16)
                emit("sync", lambda e, t=tt, s=sl, xa=s_xa[sl]: e.dma_start(
                    xa[:], d_x[t * 128:(t + 1) * 128, :]).then_inc(sems[f"xa{s}"], 16))
                wait("scalar", f"xa{sl}", xv)
                wait("vector", f"xa{sl}", xv)
                av = ln_stats(s_xa[sl][:], D, s_sq, s_st)
                wait("vector", "act", av)
                wait("vector", "pe", cnt.get(f"xn_free{sl}", 0))
                dv = bump("dve")
                emit("vector", lambda e, xn=s_xn[sl], xa=s_xa[sl], st=s_st: e.tensor_scalar(
                    xn[:], xa[:], st[:, 2:3], st[:, 3:4],
                    ALU.subtract, ALU.mult).then_inc(sems["dve"], 1))
                cnt[f"xa_free{sl}"] = dv
                wait("tensor", "dve", dv)
                for ic in range(DC):
                    half = ic % 2
                    wait("tensor", "dve", cnt.get(f"ln_h{half}", 0))
                    pv = bump("pe")
                    emit("tensor", lambda e, i=ic, h=half, xn=s_xn[sl]: e.transpose(
                        p_ln[:, h * 512: h * 512 + 128], xn[:, i * 128:(i + 1) * 128],
                        c_ident[:]).then_inc(sems["pe"], 1))
                    wait("vector", "pe", pv)
                    dv = bump("dve")
                    emit("vector", lambda e, t=tt, i=ic, h=half: e.tensor_copy(
                        s_ln1T[:, i * T + t * 128: i * T + (t + 1) * 128],
                        p_ln[:, h * 512: h * 512 + 128]).then_inc(sems["dve"], 1))
                    cnt[f"ln_h{half}"] = dv
                cnt[f"xn_free{sl}"] = cnt["pe"]

            ln1_ctx.close()
            wait("gpsimd", "wpre", cnt.get("wpre", 0))
            for ei in range(E):
                bump("wcast", 16)
                emit("gpsimd", lambda e, ei=ei: e.dma_start(d_wgb[ei], d_wg[ei]).then_inc(sems["wcast"], 16))
                bump("wcast", 16)
                emit("gpsimd", lambda e, ei=ei: e.dma_start(d_wdb[ei], d_wd[ei]).then_inc(sems["wcast"], 16))
            cnt["WCAST_ALL"] = cnt["wcast"]
            for _bi in range(4):
                cnt[f"b{_bi}_dve"] = cnt.get("dve", 0)

            bctx = contextlib.ExitStack()
            EB = bctx.enter_context
            s_Ka_ = [EB(nc.sbuf_tensor(f"s_Ka{i}", [65, T], BF16)) for i in range(2)]
            s_Kb_ = [EB(nc.sbuf_tensor(f"s_Kb{i}", [65, T], BF16)) for i in range(2)]
            s_Va_ = [EB(nc.sbuf_tensor(f"s_Va{i}", [128, NT * 65], BF16)) for i in range(2)]
            s_Vb_ = [EB(nc.sbuf_tensor(f"s_Vb{i}", [128, NT * 65], BF16)) for i in range(2)]
            s_Qa_ = [EB(nc.sbuf_tensor(f"s_Qa{i}", [65, QG * 128], BF16)) for i in range(2)]
            s_Qb_ = [EB(nc.sbuf_tensor(f"s_Qb{i}", [65, QG * 128], BF16)) for i in range(2)]
            s_PT = [EB(nc.sbuf_tensor(f"s_PT{i}", [128, QG * 128], BF16)) for i in range(3)]
            s_Op = EB(nc.sbuf_tensor("s_Op", [128, T], BF16))
            s_mst = EB(nc.sbuf_tensor("s_mst", [128, 64], F32))
            s_m = EB(nc.sbuf_tensor("s_m", [128, 4 * QG], F32))
            s_rl = [EB(nc.sbuf_tensor(f"s_rl{i}", [1, QG * 128], BF16)) for i in range(2)]
            s_bc = EB(nc.sbuf_tensor("s_bc", [128, QG * 128], BF16))

            p_s = EB(nc.psum_tensor("p_s", [128, 1024], F32))
            p_st = EB(nc.psum_tensor("p_st", [128, 1024], F32))
            p_oa = EB(nc.psum_tensor("p_oa", [65, QG * 128], F32))
            p_ob = EB(nc.psum_tensor("p_ob", [65, QG * 128], F32))
            p_bc = EB(nc.psum_tensor("p_bc", [128, QG * 128], F32))
            p_misc = EB(nc.psum_tensor("p_misc", [128, 512], F32))

            NSP = T // 512
            P1B = [(p_s, 0), (p_s, 512), (p_st, 0), (p_st, 512)]

            WARM = 0

            def warm_pe(n=None):
                for _ in range(WARM if n is None else n):
                    pv = bump("pe")
                    emit("tensor", lambda e: e.matmul(
                        p_misc[:, 0:512], s_wqT[:, 0:128], s_ln1T[:, 0:512],
                        start=True, stop=True).then_inc(sems["pe"], 1))

            def bank_acquire(warm=0):
                warm_pe(warm)
                bi = cnt.get("rot", 0) % 4
                cnt["rot"] = cnt.get("rot", 0) + 1
                wait("tensor", "dve", cnt.get(f"b{bi}_dve", 0))
                wait("tensor", "act", cnt.get(f"b{bi}_act", 0))
                return bi, P1B[bi]

            if "B" in PHASES:
                wait("vector", "dve", cnt["dve"])
                for par in range(2):
                    for dst in (s_Ka_[par], s_Kb_[par]):
                        dv = bump("dve")
                        emit("vector", lambda e, d=dst: e.memset(d[64:65, :], 1.0).then_inc(sems["dve"], 1))
                    for dst in (s_Va_[par], s_Vb_[par]):
                        for ts in range(NT):
                            dv = bump("dve")
                            emit("vector", lambda e, d=dst, t=ts: e.memset(
                                d[:, t * 65 + 64: t * 65 + 65], 1.0).then_inc(sems["dve"], 1))

            def proj_tasks(pp):
                par = pp % 2
                tasks = []

                def k_task(tsp, pp=pp, par=par):
                    bi, (pb, po) = bank_acquire()
                    for ic in range(DC):
                        pv = bump("pe")
                        emit("tensor", lambda e, p=pp, i=ic, t=tsp, pb=pb, po=po: e.matmul(
                            pb[:, po: po + 512],
                            s_wkT[:, i * D + p * 128: i * D + (p + 1) * 128],
                            s_ln1T[:, i * T + t * 512: i * T + (t + 1) * 512],
                            start=(i == 0), stop=(i == DC - 1)).then_inc(sems["pe"], 1))
                    wait("scalar", "pe", cnt["pe"])
                    for dst, lo in ((s_Ka_[par], 0), (s_Kb_[par], 64)):
                        av = bump("act")
                        emit("scalar", lambda e, d=dst, l=lo, t=tsp, pb=pb, po=po: e.copy(
                            d[0:64, t * 512:(t + 1) * 512],
                            pb[l:l + 64, po: po + 512]).then_inc(sems["act"], 1))
                    cnt[f"b{bi}_act"] = cnt["act"]
                    cnt[f"kv_act{par}"] = cnt["act"]

                def v_task(ts, pp=pp, par=par):
                    bi, (pb, po) = bank_acquire()
                    for ic in range(DC):
                        pv = bump("pe")
                        emit("tensor", lambda e, p=pp, i=ic, t=ts, pb=pb, po=po: e.matmul(
                            pb[:, po: po + 128],
                            s_ln1T[:, i * T + t * 128: i * T + (t + 1) * 128],
                            s_wvT[:, i * D + p * 128: i * D + (p + 1) * 128],
                            start=(i == 0), stop=(i == DC - 1)).then_inc(sems["pe"], 1))
                    wait("scalar", "pe", cnt["pe"])
                    for dst, lo in ((s_Va_[par], 0), (s_Vb_[par], 64)):
                        av = bump("act")
                        emit("scalar", lambda e, d=dst, l=lo, t=ts, pb=pb, po=po: e.copy(
                            d[:, t * 65: t * 65 + 64],
                            pb[:, po + l: po + l + 64]).then_inc(sems["act"], 1))
                    cnt[f"b{bi}_act"] = cnt["act"]
                    cnt[f"kv_act{par}"] = cnt["act"]

                for tsp in range(NSP):
                    tasks.append(lambda tsp=tsp: k_task(tsp))
                for ts in range(NT):
                    tasks.append(lambda ts=ts: v_task(ts))
                return tasks

            proj_queue = []

            def drain_proj(n=1):
                for _ in range(n):
                    if proj_queue:
                        proj_queue.pop(0)()

            for pr in range(NP if "B" in PHASES else 0):
                par = pr % 2
                s_Ka, s_Kb = s_Ka_[par], s_Kb_[par]
                s_Va, s_Vb = s_Va_[par], s_Vb_[par]
                if pr == 0:
                    for t_ in proj_tasks(0):
                        t_()
                    cnt["kv_ready0"] = cnt["act"]
                if pr + 1 < NP:
                    proj_queue.extend(proj_tasks(pr + 1))
                wait("tensor", "act", cnt[f"kv_ready{par}"])
                VD = cnt["dve"]

                s1_queue = []
                pending_l = []

                def queue_lchain(qg0):
                    wait("vector", "pe", cnt["avp_1"])
                    wait("vector", "dve", cnt["dve"])
                    for hh, p_o in enumerate((p_oa, p_ob)):
                        dv = bump("dve")
                        emit("vector", lambda e, p_o=p_o, hh=hh: e.reciprocal(
                            s_rl[hh][:], p_o[64:65, :]).then_inc(sems["dve"], 1))
                    pending_l.append((qg0, cnt["dve"]))

                def emit_lchain():
                    if not pending_l:
                        return
                    (qg0, rlv) = pending_l.pop(0)
                    wait("tensor", "dve", rlv)
                    wait("tensor", "dve", cnt.get("bc_used", 0))
                    wait("tensor", "dve", cnt.get("mt_used", 0))
                    for hh in range(2):
                        pv = bump("pe")
                        emit("tensor", lambda e, hh=hh: e.matmul(
                            p_bc[hh * 64:(hh + 1) * 64, :], c_onesb[:, 0:64], s_rl[hh][:],
                            start=True, stop=True).then_inc(sems["pe"], 1))
                    wait("vector", "pe", cnt["pe"])
                    dv = bump("dve")
                    emit("vector", lambda e: e.tensor_copy(s_bc[:], p_bc[:]).then_inc(sems["dve"], 1))
                    cnt["bc_used"] = dv
                    wait("vector", "dve", dv)
                    wait("vector", "ost", cnt.get("op_reuse", 0))
                    for hh, p_o in enumerate((p_oa, p_ob)):
                        dv = bump("dve")
                        emit("vector", lambda e, p_o=p_o, hh=hh, qg=qg0: e.tensor_tensor(
                            s_Op[hh * 64:(hh + 1) * 64, qg * QG * 128:(qg + 1) * QG * 128],
                            p_o[0:64, :], s_bc[hh * 64:(hh + 1) * 64, :], ALU.mult).then_inc(sems["dve"], 1))
                        cnt[f"o_free_{hh}"] = dv

                def drain_any(n=1):
                    for _ in range(n):
                        if s1_queue:
                            s1_queue.pop(0)()
                        elif proj_queue:
                            proj_queue.pop(0)()

                def stage1_tasks(qg, pr):
                    par = qg % 2
                    sQa, sQb = s_Qa_[par], s_Qb_[par]
                    tasks = []

                    def qproj(qg=qg, pr=pr, par=par, sQa=sQa, sQb=sQb):
                        qslots = []
                        for j in range(QG):
                            qt = qg * QG + j
                            bi, (pb, po) = bank_acquire()
                            for ic in range(DC):
                                pv = bump("pe")
                                emit("tensor", lambda e, p=pr, i=ic, q=qt, pb=pb, po=po: e.matmul(
                                    pb[:, po: po + 128],
                                    s_wqT[:, i * D + p * 128: i * D + (p + 1) * 128],
                                    s_ln1T[:, i * T + q * 128: i * T + (q + 1) * 128],
                                    start=(i == 0), stop=(i == DC - 1)).then_inc(sems["pe"], 1))
                            qslots.append((bi, pb, po, cnt["pe"]))
                        wait("scalar", "pe", cnt.get(f"qa_used{par}", 0))
                        for j in range(QG):
                            bi, pb, po, pev = qslots[j]
                            wait("scalar", "pe", pev)
                            for dst, lo in ((sQa, 0), (sQb, 64)):
                                av = bump("act")
                                emit("scalar", lambda e, d=dst, l=lo, j=j, pb=pb, po=po: e.activation(
                                    d[0:64, j * 128:(j + 1) * 128], pb[l:l + 64, po: po + 128],
                                    AF.Copy, scale=0.125).then_inc(sems["act"], 1))
                            cnt[f"b{bi}_act"] = cnt["act"]
                        cnt[f"qev{par}"] = cnt["act"]

                    tasks.append(qproj)

                    def p1unit(j, hh, qg=qg, par=par):
                        qt = qg * QG + j
                        sQ = (s_Qa_, s_Qb_)[hh][par]
                        sK = (s_Ka_, s_Kb_)[hh][pr % 2]
                        n512 = qt // 4
                        rem = qt - 4 * n512
                        nred = 0
                        wait("tensor", "act", cnt[f"qev{par}"])
                        for ksp in range(n512 + 1):
                            bi, (pb, po) = bank_acquire()
                            last = ksp == n512
                            w = (rem + 1) * 128 if last else 512
                            k0 = ksp * 512
                            pv = bump("pe")
                            emit("tensor", lambda e, sQ=sQ, sK=sK, j=j, k0=k0, w=w,
                                 pb=pb, po=po: e.matmul(
                                pb[:, po: po + w],
                                sQ[0:64, j * 128:(j + 1) * 128],
                                sK[0:64, k0: k0 + w],
                                start=True, stop=True).then_inc(sems["pe"], 1))
                            mcol = (par * 32) + (j % 2) * 16 + hh * 8 + nred
                            wait("vector", "pe", pv)
                            if last:
                                dv = bump("dve")
                                emit("vector", lambda e, r=rem, pb=pb, po=po: e.tensor_tensor(
                                    pb[:, po + r * 128: po + (r + 1) * 128],
                                    pb[:, po + r * 128: po + (r + 1) * 128],
                                    c_masks[:], ALU.add).then_inc(sems["dve"], 1))
                                wait("vector", "dve", dv)
                            dv = bump("dve")
                            emit("vector", lambda e, n=mcol, w=w, pb=pb, po=po: e.reduce_max(
                                s_mst[:, n:n + 1], pb[:, po: po + w],
                                axis=AX).then_inc(sems["dve"], 1))
                            cnt[f"b{bi}_dve"] = dv
                            nred += 1
                        wait("vector", "dve", cnt["dve"])
                        base = (par * 32) + (j % 2) * 16 + hh * 8
                        dv = bump("dve")
                        emit("vector", lambda e, b=base, n=nred, j=j, hh=hh, par=par: e.reduce_max(
                            s_m[:, par * 2 * QG + hh * QG + j: par * 2 * QG + hh * QG + j + 1],
                            s_mst[:, b: b + n], axis=AX).then_inc(sems["dve"], 1))
                        cnt[f"comb_{par}_{hh}_{j}"] = dv

                    def mfin(j, hh, par=par):
                        sQ = (s_Qa_, s_Qb_)[hh][par]
                        slot = (hh * QG + j) % 4
                        wait("tensor", "dve", cnt[f"comb_{par}_{hh}_{j}"])
                        wait("tensor", "dve", cnt.get("bc_used", 0))
                        wait("tensor", "dve", cnt.get(f"mt_slot{slot}", 0))
                        pv = bump("pe")
                        emit("tensor", lambda e, j=j, hh=hh, par=par, sl=slot: e.transpose(
                            p_bc[0:1, sl * 128:(sl + 1) * 128],
                            s_m[:, par * 2 * QG + hh * QG + j: par * 2 * QG + hh * QG + j + 1],
                            c_ident[:]).then_inc(sems["pe"], 1))
                        wait("vector", "pe", pv)
                        dv = bump("dve")
                        emit("vector", lambda e, d=sQ, j=j, sl=slot: e.tensor_scalar(
                            d[64:65, j * 128:(j + 1) * 128], p_bc[0:1, sl * 128:(sl + 1) * 128],
                            -1.0, None, ALU.mult).then_inc(sems["dve"], 1))
                        cnt[f"mt_slot{slot}"] = dv
                        cnt[f"mdone{par}"] = dv
                        cnt["mt_used"] = dv

                    for j in range(QG):
                        for hh in range(2):
                            tasks.append(lambda j=j, hh=hh: p1unit(j, hh))
                    for j in range(QG):
                        for hh in range(2):
                            tasks.append(lambda j=j, hh=hh: mfin(j, hh))
                    return tasks

                def stage2(qg, pr):
                    par = qg % 2
                    qt_hi = qg * QG + QG - 1
                    emit_lchain()
                    for hh in range(2):
                        sQ = (s_Qa_, s_Qb_)[hh][par]
                        sK = (s_Ka_, s_Kb_)[hh][pr % 2]
                        sV = (s_Va_, s_Vb_)[hh][pr % 2]
                        p_o = (p_oa, p_ob)[hh]
                        wait("tensor", "dve", cnt.get(f"mdone{par}", 0))
                        wait("tensor", "dve", cnt.get(f"o_free_{hh}", 0))
                        pend = None
                        for kt in range(qt_hi + 1):
                            jstart = max(0, kt - qg * QG)
                            span = (QG - jstart) * 128
                            ring = kt % 3
                            bi, (pb, po) = bank_acquire()
                            pv = bump("pe")
                            emit("tensor", lambda e, sQ=sQ, sK=sK, k=kt, js=jstart, w=span,
                                 pb=pb, po=po: e.matmul(
                                pb[:, po: po + w],
                                sK[:, k * 128:(k + 1) * 128],
                                sQ[:, js * 128: js * 128 + w],
                                start=True, stop=True).then_inc(sems["pe"], 1))
                            mask_dve = None
                            if kt >= qg * QG:
                                jd = kt - qg * QG
                                wait("vector", "pe", pv)
                                mask_dve = bump("dve")
                                emit("vector", lambda e, jd=jd, js=jstart, pb=pb, po=po: e.tensor_tensor(
                                    pb[:, po + (jd - js) * 128: po + (jd - js + 1) * 128],
                                    pb[:, po + (jd - js) * 128: po + (jd - js + 1) * 128],
                                    c_maskt[:], ALU.add).then_inc(sems["dve"], 1))
                                cnt[f"b{bi}_dve"] = mask_dve
                            wait("scalar", "pe", pv)
                            if mask_dve is not None:
                                wait("scalar", "dve", mask_dve)
                            wait("scalar", "pe", cnt.get(f"pt_used{ring}", 0))
                            av = bump("act")
                            emit("scalar", lambda e, r=ring, w=span, pb=pb, po=po: e.activation(
                                s_PT[r][:, 0:w], pb[:, po: po + w],
                                AF.Exp).then_inc(sems["act"], 1))
                            cnt[f"b{bi}_act"] = av
                            if pend is not None:
                                k2, js2, w2, r2, av2 = pend
                                wait("tensor", "act", av2)
                                pv = bump("pe")
                                emit("tensor", lambda e, sV=sV, p_o=p_o, k=k2, js=js2, w=w2, r=r2,
                                     st=(k2 == 0), sp=(k2 == qt_hi): e.matmul(
                                    p_o[0:65, js * 128: js * 128 + w],
                                    sV[:, k * 65:(k + 1) * 65],
                                    s_PT[r][:, 0:w],
                                    start=st, stop=sp).then_inc(sems["pe"], 1))
                                cnt[f"pt_used{r2}"] = pv
                            pend = (kt, jstart, span, ring, av)
                            if kt % 3 == 2:
                                drain_any(1)
                        k2, js2, w2, r2, av2 = pend
                        wait("tensor", "act", av2)
                        pv = bump("pe")
                        emit("tensor", lambda e, sV=sV, p_o=p_o, k=k2, js=js2, w=w2, r=r2,
                             st=(k2 == 0): e.matmul(
                            p_o[0:65, js * 128: js * 128 + w],
                            sV[:, k * 65:(k + 1) * 65],
                            s_PT[r][:, 0:w],
                            start=st, stop=True).then_inc(sems["pe"], 1))
                        cnt[f"pt_used{r2}"] = pv
                        cnt[f"avp_{hh}"] = pv
                        if hh == 1:
                            queue_lchain(qg)
                        drain_any(1)
                    cnt[f"qa_used{par}"] = cnt["pe"]

                for t_ in stage1_tasks(0, pr):
                    t_()
                for qg in range(NQG):
                    if qg + 1 < NQG:
                        s1_queue.extend(stage1_tasks(qg + 1, pr))
                    stage2(qg, pr)
                    while s1_queue:
                        s1_queue.pop(0)()
                drain_proj(len(proj_queue))
                if pr + 1 < NP:
                    cnt[f"kv_ready{(pr + 1) % 2}"] = cnt["act"]
                emit_lchain()
                wait("scalar", "dve", cnt["dve"])
                sv = bump("ost", 16)
                emit("scalar", lambda e, p=pr: e.dma_start(
                    d_oscr[p * 128:(p + 1) * 128, :], s_Op[:]).then_inc(sems["ost"], 16))
                cnt["op_reuse"] = sv
            OST_ALL = cnt.get("ost", 0)

            bctx.close()

            wait("sync", "pe", cnt.get("pe", 0))
            wait("sync", "dve", cnt.get("dve", 0))
            wait("sync", "act", cnt.get("act", 0))
            wait("sync", "wrl", cnt.get("wrl", 0))
            s_ob = [EA(nc.sbuf_tensor(f"s_ob{i}", [128, DC * 128], BF16)) for i in range(2)]
            s_x1o = [EA(nc.sbuf_tensor(f"s_x1o{i}", [128, D], F32)) for i in range(2)]
            s_xa = [EA(nc.sbuf_tensor(f"s_xac{i}", [128, D], F32)) for i in range(2)]
            s_sq = EA(nc.sbuf_tensor("s_sqc", [128, D], BF16))
            s_st = EA(nc.sbuf_tensor("s_stc", [128, 16], F32))
            p_wo = EA(nc.psum_tensor("p_wo", [128, 2 * PDW], F32))

            d_oscr_r = d_oscr.rearrange("(c p) t -> p c t", p=128)
            for ts in range(NT if "C" in PHASES else 0):
                sl = ts % 2
                half = ts % 2
                wait("sync", "ost", OST_ALL)
                wait("sync", "pe", cnt.get(f"ob_free{sl}", 0))
                ov = bump(f"ob{sl}", 16)
                emit("sync", lambda e, t=ts, s=sl: e.dma_start(
                    s_ob[s][:].rearrange("p (c t) -> p c t", c=DC),
                    d_oscr_r[:, :, t * 128:(t + 1) * 128]).then_inc(sems[f"ob{s}"], 16))
                wait("sync", "dve", cnt.get(f"xa_free{sl}", 0))
                xv = bump(f"xa{sl}", 16)
                emit("sync", lambda e, t=ts, s=sl: e.dma_start(
                    s_xa[s][:], d_x[t * 128:(t + 1) * 128, :]).then_inc(sems[f"xa{s}"], 16))

                wait("tensor", f"ob{sl}", ov)
                wait("tensor", "dve", cnt.get(f"wo_h{half}", 0))
                wait("tensor", "act", cnt.get(f"wo_a{half}", 0))
                for ic in range(DC):
                    for n0 in range(0, D, 512):
                        w5 = min(512, D - n0)
                        pv = bump("pe")
                        emit("tensor", lambda e, s=sl, i=ic, n=n0, w=w5, h=half: e.matmul(
                            p_wo[:, h * PDW + n: h * PDW + n + w],
                            s_ob[s][:, i * 128:(i + 1) * 128],
                            s_woT[:, i * D + n: i * D + n + w],
                            start=(i == 0), stop=(i == DC - 1)).then_inc(sems["pe"], 1))
                PEWO = cnt["pe"]
                cnt[f"ob_free{sl}"] = PEWO
                wait("scalar", "pe", PEWO)
                wait("scalar", "dve", cnt["dve"])
                av = bump("act")
                emit("scalar", lambda e, h=half: e.activation(
                    s_sq[:, 0:D], p_wo[:, h * PDW: h * PDW + D], AF.Square,
                    accum_out=s_st[:, 1:2]).then_inc(sems["act"], 1))
                cnt[f"wo_a{half}"] = av
                wait("vector", "act", av)
                wait("vector", "pe", PEWO)
                dv = bump("dve")
                emit("vector", lambda e, h=half: e.reduce_sum(
                    s_st[:, 0:1], p_wo[:, h * PDW: h * PDW + D], axis=AX).then_inc(sems["dve"], 1))
                wait("vector", "dve", dv)
                dv = bump("dve")
                emit("vector", lambda e: e.tensor_scalar(s_st[:, 2:3], s_st[:, 0:1], 1.0 / D,
                                                         None, ALU.mult).then_inc(sems["dve"], 1))
                wait("vector", "dve", dv)
                dv = bump("dve")
                emit("vector", lambda e: e.scalar_tensor_tensor(
                    s_st[:, 3:4], s_st[:, 2:3], -1.0, s_st[:, 2:3], ALU.mult, ALU.mult).then_inc(sems["dve"], 1))
                wait("vector", "dve", dv)
                dv = bump("dve")
                emit("vector", lambda e: e.scalar_tensor_tensor(
                    s_st[:, 3:4], s_st[:, 1:2], 1.0 / D, s_st[:, 3:4], ALU.mult, ALU.add).then_inc(sems["dve"], 1))
                wait("scalar", "dve", dv)
                av = bump("act")
                emit("scalar", lambda e: e.activation(s_st[:, 4:5], s_st[:, 3:4], AF.Ln,
                                                      bias=EPS).then_inc(sems["act"], 1))
                wait("scalar", "act", av)
                av = bump("act")
                emit("scalar", lambda e: e.activation(s_st[:, 3:4], s_st[:, 4:5], AF.Exp,
                                                      scale=-0.5).then_inc(sems["act"], 1))
                wait("vector", "act", av)
                wait("vector", f"xa{sl}", xv)
                wait("vector", f"x1st{sl}", cnt.get(f"x1o_free{sl}", 0))
                dv = bump("dve")
                emit("vector", lambda e, s=sl, h=half: e.tensor_scalar(
                    s_x1o[s][:], p_wo[:, h * PDW: h * PDW + D], s_st[:, 2:3], s_st[:, 3:4],
                    ALU.subtract, ALU.mult).then_inc(sems["dve"], 1))
                cnt[f"wo_h{half}"] = dv
                wait("vector", "dve", dv)
                dv = bump("dve")
                emit("vector", lambda e, s=sl: e.tensor_tensor(
                    s_x1o[s][:], s_x1o[s][:], s_xa[s][:], ALU.add).then_inc(sems["dve"], 1))
                cnt[f"xa_free{sl}"] = dv
                wait("scalar", "dve", dv)
                xsv = bump(f"x1st{sl}", 16)
                emit("scalar", lambda e, t=ts, s=sl: e.dma_start(
                    d_x1[t * 128:(t + 1) * 128, :], s_x1o[s][:]).then_inc(sems[f"x1st{s}"], 16))
                cnt[f"x1o_free{sl}"] = xsv
            X1ST_ALL = (cnt.get("x1st0", 0), cnt.get("x1st1", 0))

            if "D" not in PHASES:
                wait("scalar", "dve", cnt.get("dve", 0))
                wait("scalar", "act", cnt.get("act", 0))
                yv = bump("yst0", 16)
                emit("scalar", lambda e: e.dma_start(
                    d_y[0:128, :], s_xa[0][:]).then_inc(sems["yst0"], 16))
                wait("sync", "yst0", yv)
                wait("sync", "ost", OST_ALL)
                wait("sync", "x1st0", X1ST_ALL[0])
                wait("sync", "x1st1", X1ST_ALL[1])
                run_block()
                return nc

            run_block()

        with contextlib.ExitStack() as dd:
            ED = dd.enter_context
            s_ln2T = ED(nc.sbuf_tensor("s_ln2T", [128, DC * TB], BF16))
            s_acc = ED(nc.sbuf_tensor("s_acc", [128, NTS * D], F32))
            s_act = ED(nc.sbuf_tensor("s_act", [128, NIC * TB], BF16))
            s_wg = [ED(nc.sbuf_tensor(f"s_wg{i}", [128, DC * 512], BF16)) for i in range(4)]
            s_wd = ED(nc.sbuf_tensor("s_wdn", [128, NIC * D], BF16))
            s_x1i = [ED(nc.sbuf_tensor(f"s_x1i{i}", [128, D], F32)) for i in range(2)]
            s_out = [ED(nc.sbuf_tensor(f"s_out{i}", [128, D], F32)) for i in range(2)]
            s_xn2_ = [ED(nc.sbuf_tensor(f"s_xn2{i}", [128, D], F32)) for i in range(2)]
            s_sq2_ = [ED(nc.sbuf_tensor(f"s_sq2{i}", [128, D], BF16)) for i in range(2)]
            s_st2 = ED(nc.sbuf_tensor("s_st2", [128, 96], F32))
            s_sig = ED(nc.sbuf_tensor("s_sig", [128, TB], F32))
            s_glu = [ED(nc.sbuf_tensor(f"s_glu{i}", [128, TB], F32)) for i in range(2)]
            s_uc = ED(nc.sbuf_tensor("s_uc", [128, TB], F32))
            s_w8 = ED(nc.sbuf_tensor("s_w8", [128, NTS * E], F32))
            s_wt8 = ED(nc.sbuf_tensor("s_wt8", [E, NTS * 128], BF16))
            s_sm = ED(nc.sbuf_tensor("s_sm", [128, 3 * E + 4], F32))
            p_g = ED(nc.psum_tensor("p_g", [128, 2048], F32))
            p_d = ED(nc.psum_tensor("p_d", [128, 2 * PDW], F32))

            for _eng in ("gpsimd", "sync", "scalar"):
                wait(_eng, "x1st0", X1ST_ALL[0])
                wait(_eng, "x1st1", X1ST_ALL[1])
                wait(_eng, "ost", OST_ALL)
            wait("sync", "wcast", cnt.get("WCAST_ALL", 0))
            for _s in ("dve", "act", "pool", "pe"):
                wait("tensor", _s, cnt.get(_s, 0))
                wait("vector", _s, cnt.get(_s, 0))
                wait("scalar", _s, cnt.get(_s, 0))
                wait("gpsimd", _s, cnt.get(_s, 0))

            def ln_norm(src_ap, dst_ap, dcols, sq, st):
                wait("scalar", "dve", cnt["dve"])
                wait("vector", "act", cnt["act"])
                av = bump("act")
                emit("scalar", lambda e, s=src_ap: e.activation(
                    sq[:, 0:dcols], s, AF.Square, accum_out=st[:, 1:2]).then_inc(sems["act"], 1))
                dv = bump("dve")
                emit("vector", lambda e, s=src_ap: e.reduce_sum(st[:, 0:1], s, axis=AX).then_inc(sems["dve"], 1))
                wait("vector", "dve", dv)
                wait("vector", "act", av)
                dv = bump("dve")
                emit("vector", lambda e: e.tensor_scalar(st[:, 2:3], st[:, 0:1], 1.0 / dcols,
                                                         None, ALU.mult).then_inc(sems["dve"], 1))
                wait("vector", "dve", dv)
                dv = bump("dve")
                emit("vector", lambda e: e.scalar_tensor_tensor(
                    st[:, 3:4], st[:, 2:3], -1.0, st[:, 2:3], ALU.mult, ALU.mult).then_inc(sems["dve"], 1))
                wait("vector", "dve", dv)
                dv = bump("dve")
                emit("vector", lambda e: e.scalar_tensor_tensor(
                    st[:, 3:4], st[:, 1:2], 1.0 / dcols, st[:, 3:4], ALU.mult, ALU.add).then_inc(sems["dve"], 1))
                wait("scalar", "dve", dv)
                av = bump("act")
                emit("scalar", lambda e: e.activation(st[:, 4:5], st[:, 3:4], AF.Ln, bias=EPS).then_inc(sems["act"], 1))
                wait("scalar", "act", av)
                av = bump("act")
                emit("scalar", lambda e: e.activation(st[:, 3:4], st[:, 4:5], AF.Exp, scale=-0.5).then_inc(sems["act"], 1))
                wait("vector", "act", av)
                dv = bump("dve")
                emit("vector", lambda e, s=src_ap, d=dst_ap: e.tensor_scalar(
                    d, s, st[:, 2:3], st[:, 3:4], ALU.subtract, ALU.mult).then_inc(sems["dve"], 1))
                return dv

            for tb in range(NB):
                def ln2_stats(ts):
                    sl = ts % 2
                    gt = tb * NTS + ts
                    wait("sync", "x1st0", X1ST_ALL[0])
                    wait("sync", "x1st1", X1ST_ALL[1])
                    wait("sync", "dve", cnt.get(f"x1i_free{sl}", 0))
                    xv = bump(f"xb{sl}", 16)
                    emit("sync", lambda e, g=gt, s=sl: e.dma_start(
                        s_x1i[s][:], d_x1[g * 128:(g + 1) * 128, :]).then_inc(sems[f"xb{s}"], 16))
                    wait("scalar", f"xb{sl}", xv)
                    wait("vector", f"xb{sl}", xv)
                    wait("vector", "pe", cnt.get(f"xn2_free{sl}", 0))
                    dv = ln_norm(s_x1i[sl][:], s_xn2_[sl][:], D, s_sq2_[sl],
                                 s_st2[:, sl * 16: sl * 16 + 16])
                    cnt[f"x1i_free{sl}"] = dv
                    cnt[f"ln2s_{ts}"] = dv

                def ln2_finish(ts):
                    sl = ts % 2
                    wait("tensor", "dve", cnt[f"ln2s_{ts}"])
                    for ic in range(DC):
                        wait("tensor", "dve", cnt.get("ln2_evac", 0))
                        wait("tensor", "act", cnt.get("pg_a0", 0))
                        wait("tensor", "dve", cnt.get("pg_f0", 0))
                        pv = bump("pe")
                        emit("tensor", lambda e, i=ic, s=sl: e.transpose(
                            p_g[:, 0:128], s_xn2_[s][:, i * 128:(i + 1) * 128],
                            c_ident[:]).then_inc(sems["pe"], 1))
                        wait("vector", "pe", pv)
                        cnt["ln2_evac"] = bump("dve")
                        emit("vector", lambda e, i=ic, t=ts: e.tensor_copy(
                            s_ln2T[:, i * TB + t * 128: i * TB + (t + 1) * 128],
                            p_g[:, 0:128]).then_inc(sems["dve"], 1))
                    cnt[f"xn2_free{sl}"] = cnt["pe"]
                    cnt["pg_f0"] = cnt["ln2_evac"]

                def ln2_router(ts):
                    sl = ts % 2
                    wait("tensor", "dve", cnt["ln2_evac"])
                    wait("tensor", "dve", cnt.get("pd_f0", 0))
                    for ic in range(DC):
                        pv = bump("pe")
                        emit("tensor", lambda e, i=ic, t=ts: e.matmul(
                            p_d[:, 0:E], s_ln2T[:, i * TB + t * 128: i * TB + (t + 1) * 128],
                            s_wrT[:, i * E:(i + 1) * E],
                            start=(i == 0), stop=(i == DC - 1)).then_inc(sems["pe"], 1))
                    RLG = cnt["pe"]
                    wait("vector", "pe", RLG)
                    wait("vector", "dve", cnt["dve"])
                    dv = bump("dve")
                    emit("vector", lambda e: e.tensor_tensor(
                        s_sm[:, 0:E], p_d[:, 0:E], s_brb[:, 0:E], ALU.add).then_inc(sems["dve"], 1))
                    cnt["pd_f0"] = dv
                    wait("vector", "dve", dv)
                    dv = bump("dve")
                    emit("vector", lambda e: e.reduce_max(
                        s_sm[:, E:E + 1], s_sm[:, 0:E], axis=AX, negate=True).then_inc(sems["dve"], 1))
                    wait("scalar", "dve", dv)
                    av = bump("act")
                    emit("scalar", lambda e: e.activation(
                        s_sm[:, E + 2:2 * E + 2], s_sm[:, 0:E], AF.Exp,
                        bias=s_sm[:, E:E + 1], accum_out=s_sm[:, E + 1:E + 2]).then_inc(sems["act"], 1))
                    wait("vector", "act", av)
                    dv = bump("dve")
                    emit("vector", lambda e: e.reciprocal(
                        s_sm[:, 2 * E + 2:2 * E + 3], s_sm[:, E + 1:E + 2]).then_inc(sems["dve"], 1))
                    wait("vector", "dve", dv)
                    dv = bump("dve")
                    emit("vector", lambda e: e.tensor_scalar(
                        s_sm[:, 0:E], s_sm[:, E + 2:2 * E + 2], s_sm[:, 2 * E + 2:2 * E + 3],
                        None, ALU.mult).then_inc(sems["dve"], 1))
                    wait("vector", "dve", dv)
                    dv = bump("dve")
                    emit("vector", lambda e: e.reduce_max(
                        s_sm[:, E:E + 1], s_sm[:, 0:E], axis=AX).then_inc(sems["dve"], 1))
                    wait("vector", "dve", dv)
                    dv = bump("dve")
                    emit("vector", lambda e: e.tensor_scalar(
                        s_sm[:, E + 2:2 * E + 2], s_sm[:, 0:E], s_sm[:, E:E + 1],
                        None, ALU.is_equal).then_inc(sems["dve"], 1))
                    wait("vector", "dve", dv)
                    dv = bump("dve")
                    emit("vector", lambda e: e.scalar_tensor_tensor(
                        s_sm[:, E + 2:2 * E + 2], s_sm[:, E + 2:2 * E + 2], -1e30,
                        s_sm[:, 0:E], ALU.mult, ALU.add).then_inc(sems["dve"], 1))
                    wait("vector", "dve", dv)
                    dv = bump("dve")
                    emit("vector", lambda e: e.reduce_max(
                        s_sm[:, E:E + 1], s_sm[:, E + 2:2 * E + 2], axis=AX).then_inc(sems["dve"], 1))
                    wait("vector", "dve", dv)
                    dv = bump("dve")
                    emit("vector", lambda e: e.tensor_scalar(
                        s_sm[:, E + 2:2 * E + 2], s_sm[:, 0:E], s_sm[:, E:E + 1],
                        None, ALU.is_ge).then_inc(sems["dve"], 1))
                    wait("vector", "dve", dv)
                    dv = bump("dve")
                    emit("vector", lambda e, t=ts: e.tensor_tensor(
                        s_w8[:, t * E:(t + 1) * E], s_sm[:, 0:E], s_sm[:, E + 2:2 * E + 2],
                        ALU.mult).then_inc(sems["dve"], 1))
                    wait("tensor", "dve", dv)
                    wait("tensor", "dve", cnt.get("pg_f0", 0))
                    pv = bump("pe")
                    emit("tensor", lambda e, t=ts: e.transpose(
                        p_g[0:E, 0:128], s_w8[:, t * E:(t + 1) * E], c_ident[:]).then_inc(sems["pe"], 1))
                    wait("vector", "pe", pv)
                    dv = bump("dve")
                    emit("vector", lambda e, t=ts: e.tensor_copy(
                        s_wt8[0:E, t * 128:(t + 1) * 128], p_g[0:E, 0:128]).then_inc(sems["dve"], 1))
                    cnt["pg_f0"] = dv
                    wait("tensor", "dve", dv)
                    wait("tensor", "dve", cnt.get("pd_f1", 0))
                    wait("vector", "dve", cnt.get(f"acc_used_{ts}", 0))
                    for n0 in range(0, D, 512):
                        w5 = min(512, D - n0)
                        pv = bump("pe")
                        emit("tensor", lambda e, t=ts, n=n0, w=w5: e.matmul(
                            p_d[:, PDW + n: PDW + n + w],
                            s_wt8[0:E, t * 128:(t + 1) * 128],
                            s_dbb[0:E, n:n + w],
                            start=True, stop=True).then_inc(sems["pe"], 1))
                    wait("vector", "pe", cnt["pe"])
                    dv = bump("dve")
                    emit("vector", lambda e, t=ts: e.tensor_copy(
                        s_acc[:, t * D:(t + 1) * D], p_d[:, PDW:PDW + D]).then_inc(sems["dve"], 1))
                    cnt[f"acc_v_{ts}"] = dv
                    cnt["pd_f1"] = dv

                for ts in range(NTS):
                    ln2_stats(ts)
                    ln2_finish(ts)
                    ln2_router(ts)

                def ln3_fin(ts):
                    sl = ts % 2
                    gt = tb * NTS + ts
                    wait("sync", "dve", cnt.get(f"x1i_free{sl}", 0))
                    xv = bump(f"xb{sl}", 16)
                    emit("sync", lambda e, g=gt, s=sl: e.dma_start(
                        s_x1i[s][:], d_x1[g * 128:(g + 1) * 128, :]).then_inc(sems[f"xb{s}"], 16))
                    wait("vector", "dve", cnt[f"acc_v_{ts}"])
                    wait("scalar", "dve", cnt[f"acc_v_{ts}"])
                    wait("vector", f"yst{sl}", cnt.get(f"out_free{sl}", 0))
                    dv = ln_norm(s_acc[:, ts * D:(ts + 1) * D], s_out[sl][:], D,
                                 s_sq2_[sl], s_st2[:, sl * 16: sl * 16 + 16])
                    cnt[f"acc_used_{ts}"] = dv
                    wait("vector", "dve", dv)
                    wait("vector", f"xb{sl}", xv)
                    dv = bump("dve")
                    emit("vector", lambda e, s=sl: e.tensor_tensor(
                        s_out[s][:], s_out[s][:], s_x1i[s][:], ALU.add).then_inc(sems["dve"], 1))
                    cnt[f"x1i_free{sl}"] = dv
                    wait("scalar", "dve", dv)
                    yv = bump(f"yst{sl}", 16)
                    emit("scalar", lambda e, g=gt, s=sl: e.dma_start(
                        d_y[g * 128:(g + 1) * 128, :], s_out[s][:]).then_inc(sems[f"yst{s}"], 16))
                    cnt[f"out_free{sl}"] = yv

                ln3_done = set()

                for ei in range(E if ("D2" in PHASES or PHASES == "ABCD") else 0):
                    wait("scalar", "pe", cnt.get("wdn_free", 0))
                    wv = bump("wd0", 16)
                    emit("scalar", lambda e, ei=ei: e.dma_start(
                        s_wd[:].rearrange("p (c h) -> p c h", c=NIC),
                        d_wdb[ei].rearrange("(c p) h -> p c h", p=128)).then_inc(sems["wd0"], 16))
                    cnt["wdn_last"] = wv

                    for sl_i in range(NSL):
                        buf = (ei * NSL + sl_i) % 4
                        sname = f"wg{buf}"
                        wait("sync", "pe", cnt.get(f"wg_free{buf}", 0))
                        wv1 = bump(sname, 16)
                        emit("sync", lambda e, ei=ei, s=sl_i, b=buf, sm=sems[sname]: e.dma_start(
                            s_wg[b][:, 0:DC * 256].rearrange("p (c n) -> p c n", n=256),
                            d_wgb[ei, :, s * 256:(s + 1) * 256].rearrange("(c p) n -> p c n", p=128)
                        ).then_inc(sm, 16))
                        wv2 = bump(sname, 16)
                        emit("sync", lambda e, ei=ei, s=sl_i, b=buf, sm=sems[sname]: e.dma_start(
                            s_wg[b][:, DC * 256:DC * 512].rearrange("p (c n) -> p c n", n=256),
                            d_wgb[ei, :, I + s * 256: I + (s + 1) * 256].rearrange("(c p) n -> p c n", p=128)
                        ).then_inc(sm, 16))

                        wait("tensor", sname, wv2)
                        for grp in range(4):
                            isup = grp >= 2
                            nn = grp % 2
                            nchunk = sl_i * 2 + nn
                            bcol = ei * NCH + (NIC + nchunk if isup else nchunk)
                            col0 = (DC * 256 if isup else 0) + nn * 128
                            for t5 in range(TT5):
                                bi = cnt.get("gu_rot", 0) % 4
                                cnt["gu_rot"] = cnt.get("gu_rot", 0) + 1
                                wait("tensor", "dve", cnt.get(f"pg_f{bi}", 0))
                                wait("tensor", "act", cnt.get(f"pg_a{bi}", 0))
                                for ic in range(DC):
                                    pv = bump("pe")
                                    emit("tensor", lambda e, b=buf, i=ic, t=t5, c0=col0, bi=bi: e.matmul(
                                        p_g[:, bi * 512:(bi + 1) * 512],
                                        s_wg[b][:, c0 + i * 256: c0 + i * 256 + 128],
                                        s_ln2T[:, i * TB + t * 512: i * TB + (t + 1) * 512],
                                        start=(i == 0), stop=(i == DC - 1)).then_inc(sems["pe"], 1))
                                GUP = cnt["pe"]
                                if "e" not in EXPERT_PARTS:
                                    continue
                                if not isup:
                                    wait("scalar", "pe", GUP)
                                    wait("scalar", "dve", W_PREP_DVE)
                                    wait("scalar", "pool", cnt.get(f"sig_used_{t5}", 0))
                                    av = bump("act")
                                    emit("scalar", lambda e, bi=bi, bc=bcol, t=t5: e.activation(
                                        s_sig[:, t * 512:(t + 1) * 512],
                                        p_g[:, bi * 512:(bi + 1) * 512], AF.Sigmoid,
                                        scale=ALPHA, bias=s_gubs[:, bc:bc + 1]).then_inc(sems["act"], 1))
                                    cnt[f"pg_a{bi}"] = av
                                    wait("vector", "pe", GUP)
                                    wait("vector", "act", av)
                                    wait("vector", "pool", cnt.get(f"uc_used_{t5}", 0))
                                    dv = bump("dve")
                                    emit("vector", lambda e, bi=bi, bc=bcol, t=t5: e.tensor_scalar(
                                        s_uc[:, t * 512:(t + 1) * 512],
                                        p_g[:, bi * 512:(bi + 1) * 512],
                                        s_gub[:, bc:bc + 1], LIMIT, ALU.add, ALU.min).then_inc(sems["dve"], 1))
                                    cnt[f"pg_f{bi}"] = dv
                                    wait("gpsimd", "dve", dv)
                                    wait("gpsimd", "act", av)
                                    wait("gpsimd", "pool", cnt.get(f"glu_used_{nn}_{t5}", 0))
                                    lv = bump("pool")
                                    emit("gpsimd", lambda e, n=nn, t=t5: e.tensor_tensor(
                                        s_glu[n][:, t * 512:(t + 1) * 512],
                                        s_uc[:, t * 512:(t + 1) * 512],
                                        s_sig[:, t * 512:(t + 1) * 512],
                                        ALU.mult).then_inc(sems["pool"], 1))
                                    cnt[f"glu_{nn}_{t5}"] = lv
                                    cnt[f"uc_used_{t5}"] = lv
                                    cnt[f"sig_used_{t5}"] = lv
                                else:
                                    wait("vector", "pe", GUP)
                                    wait("vector", "pool", cnt.get(f"uc_used_{t5}", 0))
                                    dv = bump("dve")
                                    emit("vector", lambda e, bi=bi, bc=bcol, t=t5: e.tensor_scalar(
                                        s_uc[:, t * 512:(t + 1) * 512],
                                        p_g[:, bi * 512:(bi + 1) * 512],
                                        s_gubp[:, bc:bc + 1], LIMIT + 1.0, ALU.add, ALU.min).then_inc(sems["dve"], 1))
                                    cnt[f"pg_f{bi}"] = dv
                                    cnt[f"pg_a{bi}"] = cnt.get("act", 0)
                                    wait("vector", "dve", dv)
                                    dv = bump("dve")
                                    emit("vector", lambda e, t=t5: e.tensor_scalar(
                                        s_uc[:, t * 512:(t + 1) * 512],
                                        s_uc[:, t * 512:(t + 1) * 512], -(LIMIT - 1.0), None,
                                        ALU.max).then_inc(sems["dve"], 1))
                                    wait("gpsimd", "dve", dv)
                                    wait("gpsimd", "pool", cnt[f"glu_{nn}_{t5}"])
                                    wait("gpsimd", "pe", cnt.get("wdn_free", 0))
                                    lv = bump("pool")
                                    emit("gpsimd", lambda e, n=nn, na=nchunk, t=t5: e.tensor_tensor(
                                        s_act[:, na * TB + t * 512: na * TB + (t + 1) * 512],
                                        s_uc[:, t * 512:(t + 1) * 512],
                                        s_glu[n][:, t * 512:(t + 1) * 512],
                                        ALU.mult).then_inc(sems["pool"], 1))
                                    cnt[f"uc_used_{t5}"] = lv
                                    cnt[f"glu_used_{nn}_{t5}"] = lv
                                    cnt[f"actc_{nchunk}"] = lv
                        cnt[f"wg_free{buf}"] = cnt["pe"]

                    wait("tensor", "wd0", cnt["wdn_last"])
                    for ts in range(NTS if "d" in EXPERT_PARTS else 0):
                        dsel = ts % 2
                        wait("tensor", "dve", cnt.get(f"pd_f{dsel}", 0))
                        for ncx in range(NIC):
                            wait("tensor", "pool", cnt.get(f"actc_{ncx}", 0))
                            for n0 in range(0, D, 512):
                                w5 = min(512, D - n0)
                                pv = bump("pe")
                                emit("tensor", lambda e, c=ncx, t=ts, n=n0, w=w5, d=dsel: e.matmul(
                                    p_d[:, d * PDW + n: d * PDW + n + w],
                                    s_act[:, c * TB + t * 128: c * TB + (t + 1) * 128],
                                    s_wd[:, c * D + n: c * D + n + w],
                                    start=(c == 0), stop=(c == NIC - 1)).then_inc(sems["pe"], 1))
                        DNP = cnt["pe"]
                        wait("vector", "pe", DNP)
                        wait("vector", "dve", cnt[f"acc_v_{ts}"])
                        dv = bump("dve")
                        emit("vector", lambda e, t=ts, d=dsel, ei=ei: e.scalar_tensor_tensor(
                            s_acc[:, t * D:(t + 1) * D], p_d[:, d * PDW: d * PDW + D],
                            s_w8[:, t * E + ei: t * E + ei + 1], s_acc[:, t * D:(t + 1) * D],
                            ALU.mult, ALU.add).then_inc(sems["dve"], 1))
                        cnt[f"acc_v_{ts}"] = dv
                        cnt[f"pd_f{dsel}"] = dv
                        if False and ei == E - 1 and PHASES in ("ABCD", "ABCD3"):
                            ln3_fin(ts)
                            ln3_done.add(ts)
                    cnt["wdn_free"] = cnt["pe"]

                if PHASES in ("ABCD", "ABCD3"):
                    for ts in range(NTS):
                        st = s_st2[:, 32 + ts * 8: 32 + ts * 8 + 8]
                        src_ap = s_acc[:, ts * D:(ts + 1) * D]
                        wait("scalar", "dve", cnt[f"acc_v_{ts}"])
                        wait("vector", "dve", cnt[f"acc_v_{ts}"])
                        wait("scalar", "act", cnt.get(f"l3a_{ts - 2}", 0) if ts >= 2 else cnt.get("act", 0) if ts == 0 else 0)
                        av = bump("act")
                        emit("scalar", lambda e, s=src_ap, st=st, sq=s_sq2_[ts % 2]: e.activation(
                            sq[:, 0:D], s, AF.Square, accum_out=st[:, 1:2]).then_inc(sems["act"], 1))
                        cnt[f"l3a_{ts}"] = av
                        dv = bump("dve")
                        emit("vector", lambda e, s=src_ap, st=st: e.reduce_sum(
                            st[:, 0:1], s, axis=AX).then_inc(sems["dve"], 1))
                        cnt[f"l3d_{ts}"] = dv
                    for ts in range(NTS):
                        st = s_st2[:, 32 + ts * 8: 32 + ts * 8 + 8]
                        wait("vector", "dve", cnt[f"l3d_{ts}"])
                        wait("vector", "act", cnt[f"l3a_{ts}"])
                        dv = bump("dve")
                        emit("vector", lambda e, st=st: e.tensor_scalar(
                            st[:, 2:3], st[:, 0:1], 1.0 / D, None, ALU.mult).then_inc(sems["dve"], 1))
                        wait("vector", "dve", dv)
                        dv = bump("dve")
                        emit("vector", lambda e, st=st: e.scalar_tensor_tensor(
                            st[:, 3:4], st[:, 2:3], -1.0, st[:, 2:3], ALU.mult, ALU.mult).then_inc(sems["dve"], 1))
                        wait("vector", "dve", dv)
                        dv = bump("dve")
                        emit("vector", lambda e, st=st: e.scalar_tensor_tensor(
                            st[:, 3:4], st[:, 1:2], 1.0 / D, st[:, 3:4], ALU.mult, ALU.add).then_inc(sems["dve"], 1))
                        cnt[f"l3v_{ts}"] = dv
                    for ts in range(NTS):
                        st = s_st2[:, 32 + ts * 8: 32 + ts * 8 + 8]
                        wait("scalar", "dve", cnt[f"l3v_{ts}"])
                        av = bump("act")
                        emit("scalar", lambda e, st=st: e.activation(
                            st[:, 4:5], st[:, 3:4], AF.Ln, bias=EPS).then_inc(sems["act"], 1))
                        wait("scalar", "act", av)
                        av = bump("act")
                        emit("scalar", lambda e, st=st: e.activation(
                            st[:, 3:4], st[:, 4:5], AF.Exp, scale=-0.5).then_inc(sems["act"], 1))
                        cnt[f"l3r_{ts}"] = av
                    for ts in range(NTS):
                        sl = ts % 2
                        gt = tb * NTS + ts
                        st = s_st2[:, 32 + ts * 8: 32 + ts * 8 + 8]
                        wait("sync", "dve", cnt.get(f"x1i_free{sl}", 0))
                        xv = bump(f"xb{sl}", 16)
                        emit("sync", lambda e, g=gt, s=sl: e.dma_start(
                            s_x1i[s][:], d_x1[g * 128:(g + 1) * 128, :]).then_inc(sems[f"xb{s}"], 16))
                        wait("vector", "act", cnt[f"l3r_{ts}"])
                        wait("vector", f"yst{sl}", cnt.get(f"out_free{sl}", 0))
                        dv = bump("dve")
                        emit("vector", lambda e, s=sl, st=st, ts=ts: e.tensor_scalar(
                            s_out[s][:], s_acc[:, ts * D:(ts + 1) * D], st[:, 2:3], st[:, 3:4],
                            ALU.subtract, ALU.mult).then_inc(sems["dve"], 1))
                        cnt[f"acc_used_{ts}"] = dv
                        wait("vector", "dve", dv)
                        wait("vector", f"xb{sl}", xv)
                        dv = bump("dve")
                        emit("vector", lambda e, s=sl: e.tensor_tensor(
                            s_out[s][:], s_out[s][:], s_x1i[s][:], ALU.add).then_inc(sems["dve"], 1))
                        cnt[f"x1i_free{sl}"] = dv
                        wait("scalar", "dve", dv)
                        yv = bump(f"yst{sl}", 16)
                        emit("scalar", lambda e, g=gt, s=sl: e.dma_start(
                            d_y[g * 128:(g + 1) * 128, :], s_out[s][:]).then_inc(sems[f"yst{s}"], 16))
                        cnt[f"out_free{sl}"] = yv
                ln3_done.clear()

            if PHASES != "ABCD":
                wait("scalar", "dve", cnt["dve"])
                wait("scalar", "act", cnt["act"])
                wait("scalar", "pool", cnt.get("pool", 0))
                yv = bump("yst0", 16)
                emit("scalar", lambda e: e.dma_start(
                    d_y[0:128, :], s_x1i[0][:]).then_inc(sems["yst0"], 16))
            wait("sync", "yst0", cnt.get("yst0", 0))
            wait("sync", "yst1", cnt.get("yst1", 0))
            for s in ("wd0", "wg0", "wg1", "wg2", "wg3"):
                wait("sync", s, cnt.get(s, 0))
            run_block()

    return nc


def _consts():
    ident = np.eye(128, dtype=np.float32)
    iu = np.triu(np.ones((128, 128), dtype=np.float32), 1)
    return dict(
        c_ident=ident,
        c_masks=np.ascontiguousarray(iu * NEG),
        c_maskt=np.ascontiguousarray(iu.T * NEG),
        c_ones=np.ones((1, 128), dtype=np.float32),
        c_onesb=np.ones((1, 128), dtype=ml_dtypes.bfloat16),
        c_eps=np.full((128, 1), EPS, dtype=np.float32),
        c_sel=np.concatenate([
            np.concatenate([np.ones((1, 64)), np.zeros((1, 64))], 1),
            np.concatenate([np.zeros((1, 64)), np.ones((1, 64))], 1)], 0).astype(ml_dtypes.bfloat16),
    )


def kernel(L0_attn_x, L0_attn_Wq, L0_attn_Wk, L0_attn_Wv, L0_attn_Wo,
           L0_ffn_router_weight, L0_ffn_router_bias, L0_ffn_gate_up_proj,
           L0_ffn_gate_up_bias, L0_ffn_down_proj, L0_ffn_down_bias):
    cfg = CFG()
    x = np.ascontiguousarray(np.asarray(L0_attn_x, dtype=np.float32))
    B = x.shape[0]
    base = dict(
        wq=np.ascontiguousarray(L0_attn_Wq, dtype=np.float32),
        wk=np.ascontiguousarray(L0_attn_Wk, dtype=np.float32),
        wv=np.ascontiguousarray(L0_attn_Wv, dtype=np.float32),
        wo=np.ascontiguousarray(L0_attn_Wo, dtype=np.float32),
        wr=np.ascontiguousarray(L0_ffn_router_weight, dtype=np.float32),
        br=np.ascontiguousarray(np.asarray(L0_ffn_router_bias, dtype=np.float32).reshape(1, -1)),
        wg=np.ascontiguousarray(L0_ffn_gate_up_proj, dtype=np.float32),
        bg=np.ascontiguousarray(L0_ffn_gate_up_bias, dtype=np.float32),
        wd=np.ascontiguousarray(L0_ffn_down_proj, dtype=np.float32),
        bd=np.ascontiguousarray(L0_ffn_down_bias, dtype=np.float32),
    )
    base.update(_consts())
    nc = build_nc(cfg)
    in_maps = [dict(base, x=np.ascontiguousarray(x[c])) for c in range(B)]
    res = run_bass_kernel_spmd(nc, in_maps, core_ids=list(range(B)))
    return np.stack([np.asarray(res.results[c]["y"]) for c in range(B)], axis=0)
